# version 02
# baseline (speedup 1.0000x reference)
import numpy as np
import ml_dtypes
from contextlib import ExitStack

import concourse.bass as bass
import concourse.mybir as mybir
from concourse.bass_utils import run_bass_kernel_spmd

F32 = mybir.dt.float32
BF16 = mybir.dt.bfloat16
AF = mybir.ActivationFunctionType
ALU = mybir.AluOpType
AX = mybir.AxisListType

ENGINES = ("sync", "scalar", "vector", "tensor", "gpsimd")

SEMS = ["s_t", "s_v", "s_sc", "s_g", "s_cc", "s_xa",
        "q_sm", "q_id", "q_wc0", "q_wc1", "q_wc2", "q_wc3", "q_ga", "q_gb",
        "q_xnT", "q_nL", "q_Pp", "q_xa", "q_w1a", "q_w1b", "q_w2a", "q_w2b",
        "g_L", "g_P", "g_X", "g_wqa", "g_wqb", "g_cca", "g_ccb", "g_recv", "g_out"]


class Prog:

    def __init__(self):
        self.recs = {e: [] for e in ENGINES}
        self.cnt = {}
        self.events = {}

    def emit(self, eng, fn, inc=None):
        self.recs[eng].append(("op", fn, inc))
        if inc is not None:
            self.cnt[inc[0]] = self.cnt.get(inc[0], 0) + inc[1]

    def wait(self, eng, sem, event):
        self.recs[eng].append(("wait", sem, event))

    def mark(self, name, sem):
        self.events[name] = self.cnt.get(sem, 0)

    def replay(self, block, semh):
        for ename in ENGINES:
            recs = self.recs[ename]
            if not recs:
                continue

            def body(eng, recs=recs):
                for r in recs:
                    if r[0] == "op":
                        inst = r[1](eng)
                        if r[2] is not None:
                            inst.then_inc(semh[r[2][0]], r[2][1])
                    else:
                        _, sem, ev = r
                        v = self.events[ev] if isinstance(ev, str) else ev
                        if v > 0:
                            eng.wait_ge(semh[sem], v)

            getattr(block, ename)(body)


def build_program(D=1024, FF=1024 * 4, NT=4096, C=512, debug=False):
    assert FF == 4 * D and C == 512
    ndt = D // 128
    nft = FF // 128
    TT = C // 128
    NC = NT // C
    G1 = 4
    NG1 = nft // G1
    NW2G = nft // 4
    nq = ndt + NW2G
    NEL1 = FF * D
    CEPS = 1e-8
    CHI = float(1.0 - 1e-8)

    nc = bass.Bass("TRN2", target_bir_lowering=False, detect_race_conditions=False)

    x_h = nc.dram_tensor("x", [NT, D], F32, kind="ExternalInput").ap()
    wfT_h = nc.dram_tensor("wfT", [D, D], F32, kind="ExternalInput").ap()
    wiT_h = nc.dram_tensor("wiT", [D, D], F32, kind="ExternalInput").ap()
    w1T_h = nc.dram_tensor("w1T", [D, FF], F32, kind="ExternalInput").ap()
    w2T_h = nc.dram_tensor("w2T", [FF, D], F32, kind="ExternalInput").ap()
    NSM = 3 * ndt + nft + 16 + 128
    sm_h = nc.dram_tensor("smalls", [128, NSM], F32, kind="ExternalInput").ap()
    idb_h = nc.dram_tensor("identb", [128, 128], BF16, kind="ExternalInput").ap()
    out_h = nc.dram_tensor("out", [NT, D], F32, kind="ExternalOutput").ap()

    kd_ = "ExternalOutput" if debug else "Internal"
    if debug:
        dbg_wabs = nc.dram_tensor("dbg_wabs", [128, ndt + NW2G], F32, kind="ExternalOutput").ap()
        dbg_gbc = nc.dram_tensor("dbg_gbc", [128, 2], F32, kind="ExternalOutput").ap()
        dbg_thr = nc.dram_tensor("dbg_thr", [128, 4], F32, kind="ExternalOutput").ap()
        dbg_vtmp = nc.dram_tensor("dbg_vtmp", [128, 6 * (C // 128)], F32, kind="ExternalOutput").ap()
        dbg_mcol = nc.dram_tensor("dbg_mcol", [128, (C // 128)], F32, kind="ExternalOutput").ap()
        dbg_sqcol = nc.dram_tensor("dbg_sqcol", [128, (C // 128)], F32, kind="ExternalOutput").ap()
    scL = nc.dram_tensor("scL", [NC, 128, ndt * C], F32, kind=kd_)
    scP = nc.dram_tensor("scP", [NC, 128, ndt * C], F32, kind=kd_)
    scX = nc.dram_tensor("scX", [NC, 128, ndt * C], BF16, kind=kd_)
    scq1 = nc.dram_tensor("scq1", [128, nft, ndt, 128], BF16, kind=kd_)
    scq2 = nc.dram_tensor("scq2", [128, nft, D], BF16, kind=kd_)
    ccs = nc.dram_tensor("ccs", [128, 2 * ndt], F32)
    ccr = nc.dram_tensor("ccr", [8 * 128, 2 * ndt], F32)

    P = Prog()
    with ExitStack() as stack:
        e = stack.enter_context
        wf = e(nc.sbuf_tensor("wf", [128, max(ndt * D, 2 * G1 * ndt * 128)], BF16))
        wi = e(nc.sbuf_tensor("wi", [128, max(ndt * D, 2 * nft * 128)], BF16))
        xa = e(nc.sbuf_tensor("xa", [128, TT * D], F32))
        xnb = e(nc.sbuf_tensor("xnb", [128, TT * D], BF16))
        xnT = e(nc.sbuf_tensor("xnT", [128, ndt * C], BF16))
        nL = e(nc.sbuf_tensor("nL", [128, ndt * C], F32))
        Pp = e(nc.sbuf_tensor("Pp", [128, ndt * C], F32))
        hsb = e(nc.sbuf_tensor("hsb", [128, nft * C], BF16))
        gsb = e(nc.sbuf_tensor("gsb", [128, ndt * C], F32))
        gstage = e(nc.sbuf_tensor("gstage", [128, ndt * C], F32))
        fa = [e(nc.sbuf_tensor(f"fa{i}", [128, C], F32)) for i in range(2)]
        ia = [e(nc.sbuf_tensor(f"ia{i}", [128, C], F32)) for i in range(2)]
        nb = e(nc.sbuf_tensor("nb", [128, C], F32))
        ab = e(nc.sbuf_tensor("ab", [128, C], F32))
        sqb = e(nc.sbuf_tensor("sqb", [128, C], F32))
        smalls = e(nc.sbuf_tensor("smalls_sb", [128, 3 * ndt + nft + 16 + 128], F32))
        idb = e(nc.sbuf_tensor("idb", [128, 128], BF16))
        lcar = e(nc.sbuf_tensor("lcar", [128, ndt], F32))
        pcar = e(nc.sbuf_tensor("pcar", [128, ndt], F32))
        ncar2 = [e(nc.sbuf_tensor(f"ncar{i}", [128, ndt], F32)) for i in range(2)]
        ncar = ncar2[0]
        tmp2 = e(nc.sbuf_tensor("tmp2", [128, ndt], F32))
        tmpA = e(nc.sbuf_tensor("tmpA", [128, ndt], F32))
        tmpB = e(nc.sbuf_tensor("tmpB", [128, ndt], F32))
        recv = e(nc.sbuf_tensor("recv", [128, 8 * 2 * ndt], F32))
        ones = e(nc.sbuf_tensor("ones", [128, 128], F32))
        wabs = e(nc.sbuf_tensor("wabs", [128, nq], F32))
        sums = e(nc.sbuf_tensor("sums", [128, 2], F32))
        gbc = e(nc.sbuf_tensor("gbc", [128, 2], F32))
        thr = e(nc.sbuf_tensor("thr", [128, 4], F32))
        epst = e(nc.sbuf_tensor("epst", [128, 1], F32))
        mcol = e(nc.sbuf_tensor("mcol", [128, TT], F32))
        sqcol = e(nc.sbuf_tensor("sqcol", [128, TT], F32))
        vtmp = e(nc.sbuf_tensor("vtmp", [128, 6 * TT], F32))

        bfs = smalls[:, 0:ndt]
        bis = smalls[:, ndt:2 * ndt]
        b2s = smalls[:, 2 * ndt:3 * ndt]
        b1s = smalls[:, 3 * ndt:3 * ndt + nft]
        cms = smalls[:, 3 * ndt + nft:3 * ndt + nft + 8]
        cm1s = smalls[:, 3 * ndt + nft + 8:3 * ndt + nft + 16]
        idf = smalls[:, 3 * ndt + nft + 16:3 * ndt + nft + 16 + 128]

        psA = [e(nc.psum_tensor(f"psA{i}", [128, C], F32)) for i in range(4)]
        psTa = [e(nc.psum_tensor(f"psTa{i}", [128, C], BF16)) for i in range(2)]
        psTb = e(nc.psum_tensor("psTb", [128, ndt * 128], F32))

        semh = {n: e(nc.semaphore(n)) for n in SEMS}
        block = e(nc.Block())

        def col(t, j):
            return t[:, j:j + 1]

        P.emit("sync", lambda g: g.dma_start(out=smalls[:, :], in_=sm_h[:, :]), ("q_sm", 16))
        P.emit("sync", lambda g: g.dma_start(out=idb[:, :], in_=idb_h[:, :]), ("q_id", 16))
        P.emit("vector", lambda g: g.memset(ones[:, :], 1.0), None)
        P.emit("vector", lambda g: g.memset(ncar2[0][:, :], 0.0), None)
        P.emit("vector", lambda g: g.memset(epst[:, :], 1e-5), ("s_v", 1))
        P.mark("v:init", "s_v")

        def emit_w_cast(name, src_h, dst_sb, k):
            quarter = k % 4
            qsem = f"q_wc{quarter}"
            st = gstage[:, quarter * D:(quarter + 1) * D]
            if k >= 4:
                P.wait("sync", "s_sc", f"sc:wcast:{name}:{k - 4}")
            elif name == "wi" and k % 4 <= ndt - 1:
                last = (k % 4) + 4 * ((ndt - 1 - (k % 4)) // 4)
                P.wait("sync", "s_sc", f"sc:wcast:wf:{last}")
            P.emit("sync", lambda g, s=src_h, k=k, st=st:
                   g.dma_start(out=st, in_=s[k * 128:(k + 1) * 128, :]), (qsem, 16))
            P.mark(f"ld:w:{name}:{k}", qsem)
            P.wait("scalar", qsem, f"ld:w:{name}:{k}")
            P.emit("scalar", lambda g, st=st, d=dst_sb, k=k:
                   g.copy(out=d[:, k * D:(k + 1) * D], in_=st), ("s_sc", 1))
            P.mark(f"sc:wcast:{name}:{k}", "s_sc")

        def stage_of(k):
            return (gsb, "q_ga") if k % 2 == 0 else (gstage, "q_gb")

        def emit_gamma(k):
            st, qsem = stage_of(k)
            junk = hsb[:, 0:nft * 128]
            if k == 1:
                P.wait("sync", "s_sc", f"sc:wcast:wi:{ndt - 1}")
            elif k >= 2:
                P.wait("sync", "s_sc", f"sc:abs:{k - 2}")
            if k < ndt:
                P.emit("sync", lambda g, k=k, st=st:
                       g.dma_start(out=st[:, :], in_=w1T_h[k * 128:(k + 1) * 128, :]), (qsem, 16))
            else:
                gg = k - ndt
                P.emit("sync", lambda g, gg=gg, st=st:
                       g.dma_start(out=st[:, :],
                                   in_=w2T_h[gg * 512:(gg + 1) * 512, :].rearrange(
                                       "(j p) d -> p j d", p=128)), (qsem, 16))
            P.mark(f"ld:gam:{k}", qsem)
            P.wait("scalar", qsem, f"ld:gam:{k}")
            P.emit("scalar", lambda g, st=st, junk=junk, k=k:
                   g.activation(out=junk, in_=st[:, :], func=AF.Abs,
                                accum_out=col(wabs, k)), ("s_sc", 1))
            P.mark(f"sc:abs:{k}", "s_sc")

        def emit_thresh():
            P.wait("vector", "s_sc", f"sc:abs:{nq - 1}")
            P.emit("vector", lambda g: g.reduce_sum(out=col(sums, 0), in_=wabs[:, 0:ndt], axis=AX.X), None)
            P.emit("vector", lambda g: g.reduce_sum(out=col(sums, 1), in_=wabs[:, ndt:nq], axis=AX.X),
                   ("s_v", 1))
            P.mark("v:sums", "s_v")
            P.wait("tensor", "s_v", "v:sums")
            P.emit("tensor", lambda g: g.matmul(psTb[:, 0:2], ones[:, :], sums[:, :],
                                                start=True, stop=True), ("s_t", 1))
            P.mark("t:gbc", "s_t")
            P.wait("vector", "s_t", "t:gbc")
            P.emit("vector", lambda g: g.tensor_copy(out=gbc[:, :], in_=psTb[:, 0:2]), ("s_v", 1))
            P.mark("v:gbc", "s_v")
            P.wait("vector", "s_v", "v:gbc")
            P.emit("vector", lambda g: g.tensor_scalar(
                out=thr[:, 0:2], in0=gbc[:, :], scalar1=0.5 / NEL1, scalar2=0.5e-5,
                op0=ALU.mult, op1=ALU.add), ("s_v", 1))
            P.mark("v:thr2", "s_v")
            P.wait("vector", "s_v", "v:thr2")
            P.emit("vector", lambda g: g.tensor_scalar(
                out=thr[:, 2:4], in0=thr[:, 0:2], scalar1=-1.0, scalar2=None,
                op0=ALU.mult), ("s_v", 1))
            P.mark("v:thr", "s_v")
            if debug:
                P.wait("gpsimd", "s_v", "v:thr")
                P.emit("gpsimd", lambda g: g.dma_start(out=dbg_wabs, in_=wabs[:, :]), ("g_out", 16))
                P.emit("gpsimd", lambda g: g.dma_start(out=dbg_gbc, in_=gbc[:, :]), ("g_out", 16))
                P.emit("gpsimd", lambda g: g.dma_start(out=dbg_thr, in_=thr[:, :]), ("g_out", 16))
                P.wait("gpsimd", "g_out", 3 * 16)

        def emit_quant(k):
            st, qsem = stage_of(k)
            wqo = hsb[:, 0:nft * 128] if k % 2 == 0 else hsb[:, 2 * nft * 128:3 * nft * 128]
            qo = hsb[:, nft * 128:2 * nft * 128] if k % 2 == 0 else hsb[:, 3 * nft * 128:4 * nft * 128]
            gwq = "g_wqa" if k % 2 == 0 else "g_wqb"
            t = 0 if k < ndt else 1
            if k < 2:
                P.wait("sync", "s_sc", f"sc:abs:{nq - 1}")
            else:
                P.wait("sync", "s_v", f"v:stt:{k - 2}")
                P.wait("sync", "s_g", f"g:q:{k - 2}")
            if k < ndt:
                P.emit("sync", lambda g, k=k, st=st:
                       g.dma_start(out=st[:, :], in_=w1T_h[k * 128:(k + 1) * 128, :]), (qsem, 16))
            else:
                gg = k - ndt
                P.emit("sync", lambda g, gg=gg, st=st:
                       g.dma_start(out=st[:, :],
                                   in_=w2T_h[gg * 512:(gg + 1) * 512, :].rearrange(
                                       "(j p) d -> p j d", p=128)), (qsem, 16))
            P.mark(f"ld:q:{k}", qsem)
            P.wait("gpsimd", qsem, f"ld:q:{k}")
            if k < 2:
                P.wait("gpsimd", "s_v", "v:thr")
            else:
                P.wait("gpsimd", "s_v", f"v:stt:{k - 2}")
            P.emit("gpsimd", lambda g, st=st, qo=qo, t=t: g.tensor_scalar(
                out=qo, in0=st[:, :], scalar1=col(thr, t), scalar2=None,
                op0=ALU.is_gt), ("s_g", 1))
            P.mark(f"g:q:{k}", "s_g")
            P.wait("vector", "s_g", f"g:q:{k}")
            if k >= 2:
                P.wait("vector", gwq, f"st:wq:{k - 2}")
            P.emit("vector", lambda g, st=st, wqo=wqo, qo=qo, t=t: g.scalar_tensor_tensor(
                out=wqo, in0=st[:, :], scalar=col(thr, 2 + t), in1=qo,
                op0=ALU.is_lt, op1=ALU.subtract), ("s_v", 1))
            P.mark(f"v:stt:{k}", "s_v")
            P.wait("gpsimd", "s_v", f"v:stt:{k}")
            if k < ndt:
                dst = scq1.ap()[:, :, k, :]
            else:
                gg = k - ndt
                dst = scq2.ap()[:, gg * 4:(gg + 1) * 4, :]
            P.emit("gpsimd", lambda g, wqo=wqo, dst=dst:
                   g.dma_start(out=dst, in_=wqo), (gwq, 16))
            P.mark(f"st:wq:{k}", gwq)

        def phaseA_chunk(c):
            if c > 0:
                P.wait("scalar", "s_v", f"v:mb:{c - 1}")
                P.emit("scalar", lambda g, c=c: g.dma_start(
                    out=xa[:, :], in_=x_h[c * C:(c + 1) * C, :].rearrange("(tt p) d -> p tt d", p=128)),
                    ("s_xa", 16))
                P.mark(f"xa:{c}", "s_xa")

            P.wait("vector", "s_xa", f"xa:{c}")
            for tt in range(TT):
                P.emit("vector", lambda g, tt=tt: g.reduce_sum(
                    out=col(mcol, tt), in_=xa[:, tt * D:(tt + 1) * D], axis=AX.X),
                    ("s_v", 1) if tt == TT - 1 else None)
            P.mark(f"v:red:{c}", "s_v")

            P.wait("scalar", "s_xa", f"xa:{c}")
            if c > 0:
                P.wait("scalar", "s_t", f"t:tp:{c - 1}")
            for tt in range(TT):
                P.emit("scalar", lambda g, tt=tt: g.activation(
                    out=xnb[:, tt * D:(tt + 1) * D], in_=xa[:, tt * D:(tt + 1) * D],
                    func=AF.Square, accum_out=col(sqcol, tt)),
                    ("s_sc", 1) if tt == TT - 1 else None)
            P.mark(f"sc:sq:{c}", "s_sc")

            P.wait("vector", "s_sc", f"sc:sq:{c}")
            P.wait("vector", "s_v", f"v:red:{c}")
            P.emit("vector", lambda g: g.tensor_scalar(
                out=vtmp[:, 0:TT], in0=mcol[:, :], scalar1=1.0 / D, scalar2=None, op0=ALU.mult),
                ("s_v", 1))
            P.mark(f"v:m:{c}", "s_v")
            P.emit("vector", lambda g: g.tensor_scalar(
                out=vtmp[:, TT:2 * TT], in0=sqcol[:, :], scalar1=1.0 / D, scalar2=None, op0=ALU.mult),
                ("s_v", 1))
            P.mark(f"v:ex2:{c}", "s_v")
            P.wait("vector", "s_v", f"v:m:{c}")
            P.emit("vector", lambda g: g.tensor_mul(
                out=vtmp[:, 2 * TT:3 * TT], in0=vtmp[:, 0:TT], in1=vtmp[:, 0:TT]), ("s_v", 1))
            P.mark(f"v:m2:{c}", "s_v")
            P.wait("vector", "s_v", f"v:m2:{c}")
            P.emit("vector", lambda g: g.scalar_tensor_tensor(
                out=vtmp[:, 3 * TT:4 * TT], in0=vtmp[:, 2 * TT:3 * TT], scalar=-1.0,
                in1=vtmp[:, TT:2 * TT], op0=ALU.mult, op1=ALU.add), ("s_v", 1))
            P.mark(f"v:var:{c}", "s_v")
            P.wait("scalar", "s_v", f"v:var:{c}")
            P.emit("scalar", lambda g: g.activation(
                out=vtmp[:, 4 * TT:5 * TT], in_=vtmp[:, 3 * TT:4 * TT], func=AF.Sqrt,
                bias=epst[:, 0:1]), ("s_sc", 1))
            P.mark(f"sc:sqrt:{c}", "s_sc")
            P.wait("vector", "s_sc", f"sc:sqrt:{c}")
            P.emit("vector", lambda g: g.reciprocal(
                out=vtmp[:, 5 * TT:6 * TT], in_=vtmp[:, 4 * TT:5 * TT]), ("s_v", 1))
            P.mark(f"v:rec:{c}", "s_v")
            P.wait("vector", "s_v", f"v:rec:{c}")
            P.emit("vector", lambda g: g.scalar_tensor_tensor(
                out=vtmp[:, 2 * TT:3 * TT], in0=vtmp[:, 0:TT], scalar=-1.0, in1=vtmp[:, 5 * TT:6 * TT],
                op0=ALU.mult, op1=ALU.mult), ("s_v", 1))
            P.mark(f"v:mb:{c}", "s_v")

            P.wait("scalar", "s_v", f"v:mb:{c}")
            for tt in range(TT):
                P.emit("scalar", lambda g, tt=tt: g.activation(
                    out=xnb[:, tt * D:(tt + 1) * D], in_=xa[:, tt * D:(tt + 1) * D],
                    func=AF.Identity, scale=col(vtmp, 5 * TT + tt), bias=col(vtmp, 2 * TT + tt)),
                    ("s_sc", 1) if tt == TT - 1 else None)
            P.mark(f"sc:xn:{c}", "s_sc")

            if c == NC - 1 and debug:
                P.wait("gpsimd", "s_sc", f"sc:xn:{c}")
                P.emit("gpsimd", lambda g: g.dma_start(out=dbg_vtmp, in_=vtmp[:, :]), ("g_out", 16))
                P.emit("gpsimd", lambda g: g.dma_start(out=dbg_mcol, in_=mcol[:, :]), ("g_out", 16))
                P.emit("gpsimd", lambda g: g.dma_start(out=dbg_sqcol, in_=sqcol[:, :]), ("g_out", 16))
                P.wait("gpsimd", "g_out", 3 * 16)

            P.wait("tensor", "s_sc", f"sc:xn:{c}")
            if c == 0:
                P.wait("tensor", "q_id", 16)
            for dt in range(ndt):
                if dt >= 2:
                    P.wait("tensor", "s_v", f"v:tpe:{c}:{dt - 2}")
                elif c > 0:
                    P.wait("tensor", "s_v", f"v:tpe:{c - 1}:{ndt - 2 + dt}")
                ps = psTa[dt % 2]
                for tt in range(TT):
                    P.emit("tensor", lambda g, tt=tt, dt=dt, ps=ps: g.transpose(
                        ps[:, tt * 128:(tt + 1) * 128],
                        xnb[:, tt * D + dt * 128: tt * D + (dt + 1) * 128], idb[:, :]),
                        ("s_t", 1) if tt == TT - 1 else None)
                P.mark(f"t:tpg:{c}:{dt}", "s_t")
                P.wait("vector", "s_t", f"t:tpg:{c}:{dt}")
                if c > 0 and dt == 0:
                    P.wait("vector", "g_X", f"st:xnT:{c - 1}")
                    P.wait("vector", "s_t", f"t:fi:{c - 1}:{ndt - 1}:i")
                P.emit("vector", lambda g, dt=dt, ps=ps: g.tensor_copy(
                    out=xnT[:, dt * C:(dt + 1) * C], in_=ps[:, :]), ("s_v", 1))
                P.mark(f"v:tpe:{c}:{dt}", "s_v")
            P.mark(f"t:tp:{c}", "s_t")

            P.wait("tensor", "s_v", f"v:tpe:{c}:{ndt - 1}")
            if c == 0:
                P.wait("tensor", "s_sc", f"sc:wcast:wi:{ndt - 1}")
                P.wait("scalar", "q_sm", 16)
            for ht in range(ndt):
                pf, pi = psA[ht % 2], psA[2 + ht % 2]
                if ht >= 2:
                    P.wait("tensor", "s_sc", f"sc:ia:{c}:{ht - 2}")
                elif c > 0:
                    P.wait("tensor", "s_sc", f"sc:ia:{c - 1}:{ndt - 2 + ht}")
                for kd in range(ndt):
                    P.emit("tensor", lambda g, kd=kd, ht=ht, pf=pf: g.matmul(
                        pf[:, :], wf[:, kd * D + ht * 128: kd * D + (ht + 1) * 128],
                        xnT[:, kd * C:(kd + 1) * C],
                        start=(kd == 0), stop=(kd == ndt - 1)),
                        ("s_t", 1) if kd == ndt - 1 else None)
                P.mark(f"t:fi:{c}:{ht}:f", "s_t")
                for kd in range(ndt):
                    P.emit("tensor", lambda g, kd=kd, ht=ht, pi=pi: g.matmul(
                        pi[:, :], wi[:, kd * D + ht * 128: kd * D + (ht + 1) * 128],
                        xnT[:, kd * C:(kd + 1) * C],
                        start=(kd == 0), stop=(kd == ndt - 1)),
                        ("s_t", 1) if kd == ndt - 1 else None)
                P.mark(f"t:fi:{c}:{ht}:i", "s_t")

                fb, ib = fa[ht % 2], ia[ht % 2]
                P.wait("scalar", "s_t", f"t:fi:{c}:{ht}:f")
                if ht >= 2:
                    P.wait("scalar", "s_v", f"v:scan:{c}:{ht - 2}")
                elif c > 0:
                    P.wait("scalar", "s_v", f"v:scan:{c - 1}:{ndt - 2 + ht}")
                P.emit("scalar", lambda g, fb=fb, ht=ht: g.activation(
                    out=fb[:, :], in_=psA[ht % 2][:, :], func=AF.Sigmoid,
                    bias=col(bfs, ht)), ("s_sc", 1))
                P.mark(f"sc:f:{c}:{ht}", "s_sc")
                P.wait("scalar", "s_t", f"t:fi:{c}:{ht}:i")
                P.emit("scalar", lambda g, ib=ib, ht=ht: g.activation(
                    out=ib[:, :], in_=psA[2 + ht % 2][:, :], func=AF.Identity,
                    bias=col(bis, ht)), ("s_sc", 1))
                P.mark(f"sc:ia:{c}:{ht}", "s_sc")

                P.wait("vector", "s_sc", f"sc:ia:{c}:{ht}")
                if ht == 0 and c > 0:
                    P.wait("vector", "g_L", f"st:L:{c - 1}")
                    P.wait("vector", "g_P", f"st:P:{c - 1}")
                P.emit("vector", lambda g, fb=fb: g.tensor_scalar(
                    out=ab[:, :], in0=fb[:, :], scalar1=CEPS, scalar2=CHI,
                    op0=ALU.max, op1=ALU.min), None)
                P.emit("vector", lambda g, ib=ib: g.scalar_tensor_tensor(
                    out=nb[:, :], in0=ab[:, :], scalar=1.0, in1=ib[:, :],
                    op0=ALU.subtract, op1=ALU.mult), None)
                initL = 0.0 if c == 0 else col(lcar, ht)
                initP = 1.0 if c == 0 else col(pcar, ht)
                P.emit("vector", lambda g, ht=ht, initL=initL: g.tensor_tensor_scan(
                    out=nL[:, ht * C:(ht + 1) * C], data0=ab[:, :], data1=nb[:, :],
                    initial=initL, op0=ALU.mult, op1=ALU.add), None)
                P.emit("vector", lambda g, ht=ht, initP=initP: g.tensor_tensor_scan(
                    out=Pp[:, ht * C:(ht + 1) * C], data0=ab[:, :], data1=ab[:, :],
                    initial=initP, op0=ALU.mult, op1=ALU.min), ("s_v", 1))
                P.mark(f"v:scan:{c}:{ht}", "s_v")

            P.emit("vector", lambda g: g.tensor_copy(
                out=lcar[:, :], in_=nL.ap().rearrange("p (h t) -> p h t", t=C)[:, :, C - 1]), None)
            P.emit("vector", lambda g: g.tensor_copy(
                out=pcar[:, :], in_=Pp.ap().rearrange("p (h t) -> p h t", t=C)[:, :, C - 1]),
                ("s_v", 1))
            P.mark(f"v:car:{c}", "s_v")

            P.wait("gpsimd", "s_v", f"v:car:{c}")
            P.emit("gpsimd", lambda g, c=c: g.dma_start(out=scL.ap()[c], in_=nL[:, :]), ("g_L", 16))
            P.mark(f"st:L:{c}", "g_L")
            P.emit("gpsimd", lambda g, c=c: g.dma_start(out=scP.ap()[c], in_=Pp[:, :]), ("g_P", 16))
            P.mark(f"st:P:{c}", "g_P")
            P.emit("gpsimd", lambda g, c=c: g.dma_start(out=scX.ap()[c], in_=xnT[:, :]), ("g_X", 16))
            P.mark(f"st:xnT:{c}", "g_X")

        def emit_cc():
            P.emit("gpsimd", lambda g: g.dma_start(out=ccs.ap()[:, 0:ndt], in_=lcar[:, :]), ("g_cca", 16))
            P.emit("gpsimd", lambda g: g.dma_start(out=ccs.ap()[:, ndt:2 * ndt], in_=pcar[:, :]), ("g_ccb", 16))
            P.wait("gpsimd", "g_cca", 16)
            P.wait("gpsimd", "g_ccb", 16)
            P.emit("gpsimd", lambda g: g.collective_compute(
                "AllGather", ALU.bypass, replica_groups=[list(range(8))],
                ins=[ccs.ap().opt()], outs=[ccr.ap().opt()]), ("s_cc", 1))
            P.wait("gpsimd", "s_cc", 1)
            P.emit("gpsimd", lambda g: g.dma_start(
                out=recv[:, :],
                in_=ccr.ap().rearrange("(r p) c -> p r c", p=128)), ("g_recv", 16))
            P.wait("vector", "g_recv", 16)
            P.wait("vector", "q_sm", 16)
            for j in range(8):
                Aj = recv[:, j * 2 * ndt + ndt:(j + 1) * 2 * ndt]
                Bj = recv[:, j * 2 * ndt: j * 2 * ndt + ndt]
                P.emit("vector", lambda g, Aj=Aj, j=j: g.tensor_scalar(
                    out=tmpA[:, :], in0=Aj, scalar1=col(cms, j), scalar2=col(cm1s, j),
                    op0=ALU.mult, op1=ALU.add), None)
                P.emit("vector", lambda g, Bj=Bj, j=j: g.tensor_scalar(
                    out=tmpB[:, :], in0=Bj, scalar1=col(cms, j), scalar2=None,
                    op0=ALU.mult), ("s_v", 1))
                P.mark(f"v:cab:{j}", "s_v")
                P.wait("vector", "s_v", f"v:cab:{j}")
                P.emit("vector", lambda g, j=j: g.tensor_mul(
                    out=tmp2[:, :], in0=ncar2[j % 2][:, :], in1=tmpA[:, :]), ("s_v", 1))
                P.mark(f"v:cm:{j}", "s_v")
                P.wait("vector", "s_v", f"v:cm:{j}")
                P.emit("vector", lambda g, j=j: g.tensor_add(
                    out=ncar2[(j + 1) % 2][:, :], in0=tmp2[:, :], in1=tmpB[:, :]), ("s_v", 1))
                P.mark(f"v:ca:{j}", "s_v")
                P.wait("vector", "s_v", f"v:ca:{j}")
            P.mark("v:ncar", "s_v")

        def phaseB_chunk(c):
            P.wait("sync", "g_X", f"st:xnT:{c}")
            if c == 0:
                P.wait("sync", "g_X", f"st:xnT:{NC - 1}")
                P.wait("sync", "g_L", f"st:L:{NC - 1}")
                P.wait("sync", "g_P", f"st:P:{NC - 1}")
                P.wait("sync", "g_cca", 16)
                P.wait("sync", "g_ccb", 16)
                P.wait("sync", "s_t", f"t:fi:{NC - 1}:{ndt - 1}:i")
            else:
                P.wait("sync", "s_t", f"t:hg:{c - 1}:{nft - 1}")
            P.emit("sync", lambda g, c=c: g.dma_start(out=xnT[:, :], in_=scX.ap()[c]), ("q_xnT", 16))
            P.mark(f"ld:BxnT:{c}", "q_xnT")
            if c > 0:
                P.wait("sync", "s_t", f"t:qt:{c - 1}:{TT - 1}")
                P.wait("sync", "s_v", f"v:sq:{c - 1}")
            P.wait("sync", "g_L", f"st:L:{c}")
            P.emit("sync", lambda g, c=c: g.dma_start(out=nL[:, :], in_=scL.ap()[c]), ("q_nL", 16))
            P.mark(f"ld:BnL:{c}", "q_nL")
            P.wait("sync", "g_P", f"st:P:{c}")
            P.emit("sync", lambda g, c=c: g.dma_start(out=Pp[:, :], in_=scP.ap()[c]), ("q_Pp", 16))
            P.mark(f"ld:BPp:{c}", "q_Pp")
            if c == 0:
                P.wait("sync", "s_sc", f"sc:xn:{NC - 1}")
                P.wait("sync", "s_v", f"v:mb:{NC - 1}")
            else:
                P.wait("sync", "s_v", f"v:cmb:{c - 1}:{TT - 1}")
            P.emit("sync", lambda g, c=c: g.dma_start(
                out=xa[:, :], in_=x_h[c * C:(c + 1) * C, :].rearrange("(tt p) d -> p tt d", p=128)),
                ("q_xa", 16))
            P.mark(f"ld:Bxa:{c}", "q_xa")
            for g1 in range(NG1):
                w1sem = "q_w1a" if g1 % 2 == 0 else "q_w1b"
                if c == 0 and g1 < 2:
                    P.wait("sync", "s_t", f"t:fi:{NC - 1}:{ndt - 1}:i")
                    P.wait("sync", "g_wqa", f"st:wq:{nq - 2}")
                    P.wait("sync", "g_wqb", f"st:wq:{nq - 1}")
                elif g1 >= 2:
                    P.wait("sync", "s_t", f"t:hg:{c}:{(g1 - 2) * G1 + G1 - 1}")
                else:
                    pg = g1 + 2 * ((NG1 - 1 - g1) // 2)
                    P.wait("sync", "s_t", f"t:hg:{c - 1}:{pg * G1 + G1 - 1}")
                P.emit("sync", lambda g, g1=g1: g.dma_start(
                    out=wf[:, (g1 % 2) * (G1 * ndt * 128):(g1 % 2 + 1) * (G1 * ndt * 128)],
                    in_=scq1.ap()[:, g1 * G1:(g1 + 1) * G1, :, :]), (w1sem, 16))
                P.mark(f"ld:wq1:{c}:{g1}", w1sem)
            for dt in range(ndt):
                w2sem = "q_w2a" if dt % 2 == 0 else "q_w2b"
                if c == 0 and dt < 2:
                    P.wait("sync", "s_t", f"t:fi:{NC - 1}:{ndt - 1}:i")
                    P.wait("sync", "g_wqa", f"st:wq:{nq - 2}")
                    P.wait("sync", "g_wqb", f"st:wq:{nq - 1}")
                elif dt >= 2:
                    P.wait("sync", "s_t", f"t:o:{c}:{dt - 2}")
                else:
                    P.wait("sync", "s_t", f"t:o:{c - 1}:{ndt - 2 + dt}")
                P.emit("sync", lambda g, dt=dt: g.dma_start(
                    out=wi[:, (dt % 2) * (nft * 128):(dt % 2 + 1) * (nft * 128)],
                    in_=scq2.ap()[:, :, dt * 128:(dt + 1) * 128]), (w2sem, 16))
                P.mark(f"ld:wq2:{c}:{dt}", w2sem)

            P.wait("tensor", "q_xnT", f"ld:BxnT:{c}")
            for m in range(nft):
                ph = psA[m % 2]
                P.wait("tensor", "q_w1a" if (m // G1) % 2 == 0 else "q_w1b", f"ld:wq1:{c}:{m // G1}")
                if m >= 2:
                    P.wait("tensor", "s_sc", f"sc:h:{c}:{m - 2}")
                elif c > 0:
                    P.wait("tensor", "s_sc", f"sc:h:{c - 1}:{nft - 2 + m}")
                else:
                    P.wait("tensor", "s_sc", f"sc:ia:{NC - 1}:{ndt - 2 + m}")
                base = ((m // G1) % 2) * (G1 * ndt * 128) + (m % G1) * ndt * 128
                for kd in range(ndt):
                    P.emit("tensor", lambda g, kd=kd, base=base, ph=ph: g.matmul(
                        ph[:, :], wf[:, base + kd * 128: base + (kd + 1) * 128],
                        xnT[:, kd * C:(kd + 1) * C],
                        start=(kd == 0), stop=(kd == ndt - 1)),
                        ("s_t", 1) if kd == ndt - 1 else None)
                P.mark(f"t:hg:{c}:{m}", "s_t")
                P.wait("scalar", "s_t", f"t:hg:{c}:{m}")
                if c == 0 and m == 0:
                    P.wait("scalar", "g_wqa", f"st:wq:{nq - 2}")
                    P.wait("scalar", "g_wqb", f"st:wq:{nq - 1}")
                    P.wait("scalar", "s_v", f"v:stt:{nq - 1}")
                P.emit("scalar", lambda g, m=m, ph=ph: g.activation(
                    out=hsb[:, m * C:(m + 1) * C], in_=ph[:, :], func=AF.Relu,
                    scale=-1.0, bias=col(b1s, m)), ("s_sc", 1))
                P.mark(f"sc:h:{c}:{m}", "s_sc")

            for dt in range(ndt):
                po = psA[2 + dt % 2]
                P.wait("tensor", "q_w2a" if dt % 2 == 0 else "q_w2b", f"ld:wq2:{c}:{dt}")
                if dt == 0:
                    P.wait("tensor", "s_sc", f"sc:h:{c}:{nft - 1}")
                if dt >= 2:
                    P.wait("tensor", "s_sc", f"sc:g:{c}:{dt - 2}")
                elif c > 0:
                    P.wait("tensor", "s_sc", f"sc:g:{c - 1}:{ndt - 2 + dt}")
                else:
                    P.wait("tensor", "s_sc", f"sc:ia:{NC - 1}:{ndt - 2 + dt}")
                wbase = (dt % 2) * (nft * 128)
                for kf in range(nft):
                    P.emit("tensor", lambda g, kf=kf, wbase=wbase, po=po: g.matmul(
                        po[:, :], wi[:, wbase + kf * 128: wbase + (kf + 1) * 128],
                        hsb[:, kf * C:(kf + 1) * C],
                        start=(kf == 0), stop=(kf == nft - 1)),
                        ("s_t", 1) if kf == nft - 1 else None)
                P.mark(f"t:o:{c}:{dt}", "s_t")
                P.wait("scalar", "s_t", f"t:o:{c}:{dt}")
                if c == 0 and dt == 0:
                    P.wait("scalar", "s_v", f"v:stt:{nq - 1}")
                if dt >= 1 or c > 0:
                    prev = (c, dt - 1) if dt >= 1 else (c - 1, ndt - 1)
                    P.wait("scalar", "s_v", f"v:qg:{prev[0]}:{prev[1]}")
                if c > 0 and dt == 0:
                    P.wait("scalar", "g_out", f"st:out:{c - 1}")
                    P.wait("scalar", "s_v", f"v:cmb:{c - 1}:{TT - 1}")
                P.emit("scalar", lambda g, dt=dt, po=po: g.activation(
                    out=gsb[:, dt * C:(dt + 1) * C], in_=po[:, :], func=AF.Silu,
                    scale=-1.0, bias=col(b2s, dt)), ("s_sc", 1))
                P.mark(f"sc:g:{c}:{dt}", "s_sc")

                P.wait("vector", "q_nL", f"ld:BnL:{c}")
                P.wait("vector", "q_Pp", f"ld:BPp:{c}")
                if c == 0 and dt == 0:
                    P.wait("vector", "s_v", "v:ncar")
                P.emit("vector", lambda g, dt=dt: g.scalar_tensor_tensor(
                    out=sqb[:, :], in0=Pp[:, dt * C:(dt + 1) * C], scalar=col(ncar, dt),
                    in1=nL[:, dt * C:(dt + 1) * C], op0=ALU.mult, op1=ALU.add),
                    ("s_v", 1) if dt == ndt - 1 else None)
                if dt == ndt - 1:
                    P.mark(f"v:sq:{c}", "s_v")
                P.wait("vector", "s_sc", f"sc:g:{c}:{dt}")
                P.emit("vector", lambda g, dt=dt: g.tensor_mul(
                    out=nL[:, dt * C:(dt + 1) * C], in0=gsb[:, dt * C:(dt + 1) * C],
                    in1=sqb[:, :]), ("s_v", 1))
                P.mark(f"v:qg:{c}:{dt}", "s_v")

            for tt in range(TT):
                P.wait("tensor", "s_v", f"v:qg:{c}:{ndt - 1}")
                if tt > 0:
                    P.wait("tensor", "s_v", f"v:cmb:{c}:{tt - 1}")
                elif c > 0:
                    P.wait("tensor", "s_v", f"v:cmb:{c - 1}:{TT - 1}")
                else:
                    P.wait("tensor", "s_v", "v:thr")
                for ht in range(ndt):
                    P.emit("tensor", lambda g, tt=tt, ht=ht: g.transpose(
                        psTb[:, ht * 128:(ht + 1) * 128],
                        nL[:, ht * C + tt * 128: ht * C + (tt + 1) * 128], idf),
                        ("s_t", 1) if ht == ndt - 1 else None)
                P.mark(f"t:qt:{c}:{tt}", "s_t")
                P.wait("vector", "s_t", f"t:qt:{c}:{tt}")
                if tt == 0:
                    P.wait("vector", "q_xa", f"ld:Bxa:{c}")
                P.emit("vector", lambda g, tt=tt: g.tensor_sub(
                    out=gsb[:, tt * D:(tt + 1) * D], in0=xa[:, tt * D:(tt + 1) * D],
                    in1=psTb[:, 0:D]), ("s_v", 1))
                P.mark(f"v:cmb:{c}:{tt}", "s_v")

            P.wait("gpsimd", "s_v", f"v:cmb:{c}:{TT - 1}")
            P.emit("gpsimd", lambda g, c=c: g.dma_start(
                out=out_h[c * C:(c + 1) * C, :].rearrange("(tt p) d -> p tt d", p=128),
                in_=gsb[:, :]), ("g_out", 16))
            P.mark(f"st:out:{c}", "g_out")

        P.emit("scalar", lambda g: g.dma_start(
            out=xa[:, :], in_=x_h[0:C, :].rearrange("(tt p) d -> p tt d", p=128)),
            ("s_xa", 16))
        P.mark("xa:0", "s_xa")
        for k in range(ndt):
            emit_w_cast("wf", wfT_h, wf, k)
        for k in range(ndt):
            emit_w_cast("wi", wiT_h, wi, k)
        prologue = ([("g", k) for k in range(nq)] + [("t", 0)]
                    + [("q", k) for k in range(nq)])
        per_slot = -(-len(prologue) // min(NC, 6))
        for c in range(NC):
            phaseA_chunk(c)
            for item in prologue[c * per_slot:(c + 1) * per_slot]:
                if item[0] == "g":
                    emit_gamma(item[1])
                elif item[0] == "t":
                    emit_thresh()
                else:
                    emit_quant(item[1])
        for item in prologue[NC * per_slot:]:
            if item[0] == "g":
                emit_gamma(item[1])
            elif item[0] == "t":
                emit_thresh()
            else:
                emit_quant(item[1])
        emit_cc()
        for c in range(NC):
            phaseB_chunk(c)

        P.recs["gpsimd"].append(("wait", "g_out", P.cnt.get("g_out", 0)))

        P.replay(block, semh)

    return nc


_NC_CACHE = {}


def _get_nc():
    if "full" not in _NC_CACHE:
        _NC_CACHE["full"] = build_program()
    return _NC_CACHE["full"]


def kernel(L0_attn_x, L0_attn_W_f, L0_attn_b_f, L0_attn_W_i, L0_attn_b_i,
           L0_ffn_W1, L0_ffn_b1, L0_ffn_W2, L0_ffn_b2):
    D, FF, NT = 1024, 4096, 4096
    ndt, nft = D // 128, FF // 128
    x = np.ascontiguousarray(np.asarray(L0_attn_x, dtype=np.float32))
    wfT = np.ascontiguousarray(np.asarray(L0_attn_W_f, dtype=np.float32).T)
    wiT = np.ascontiguousarray(np.asarray(L0_attn_W_i, dtype=np.float32).T)
    w1T = np.ascontiguousarray(np.asarray(L0_ffn_W1, dtype=np.float32).T)
    w2T = np.ascontiguousarray(np.asarray(L0_ffn_W2, dtype=np.float32).T)
    bfv = np.asarray(L0_attn_b_f, dtype=np.float32).reshape(ndt, 128).T
    biv = np.asarray(L0_attn_b_i, dtype=np.float32).reshape(ndt, 128).T
    b1v = np.asarray(L0_ffn_b1, dtype=np.float32).reshape(nft, 128).T
    b2v = np.asarray(L0_ffn_b2, dtype=np.float32).reshape(ndt, 128).T
    identb = np.eye(128, dtype=ml_dtypes.bfloat16)
    identf = np.eye(128, dtype=np.float32)

    nc = _get_nc()
    in_maps = []
    for k in range(8):
        b, s = k // 4, k % 4
        mask = np.array([1.0 if (j // 4 == k // 4 and j < k) else 0.0 for j in range(8)],
                        dtype=np.float32)
        sm = np.concatenate([
            bfv, biv, b2v, b1v,
            np.tile(mask, (128, 1)), np.tile(1.0 - mask, (128, 1)), identf,
        ], axis=1).astype(np.float32)
        in_maps.append({
            "x": np.ascontiguousarray(x[b, s * NT:(s + 1) * NT, :]),
            "wfT": wfT, "wiT": wiT, "w1T": w1T, "w2T": w2T,
            "smalls": np.ascontiguousarray(sm), "identb": identb,
        })
    res = run_bass_kernel_spmd(nc, in_maps, core_ids=list(range(8)))
    out = np.empty((2, 4 * NT, D), dtype=np.float32)
    for k in range(8):
        b, s = k // 4, k % 4
        out[b, s * NT:(s + 1) * NT, :] = res.results[k]["out"]
    return out


# version 04
# speedup vs baseline: 1.4076x; 1.4076x over previous
import numpy as np
import ml_dtypes
from contextlib import ExitStack

import concourse.bass as bass
import concourse.mybir as mybir
from concourse.bass_utils import run_bass_kernel_spmd

F32 = mybir.dt.float32
BF16 = mybir.dt.bfloat16
AF = mybir.ActivationFunctionType
ALU = mybir.AluOpType
AX = mybir.AxisListType

ENGINES = ("sync", "scalar", "vector", "tensor", "gpsimd")

SEMS = ["s_t", "s_v", "s_sc", "s_g", "s_cc", "s_xa",
        "q_sm", "q_id", "q_wc0", "q_wc1", "q_wc2", "q_wc3", "q_ga", "q_gb",
        "q_xnT", "q_nL", "q_Pp", "q_xa", "q_w1a", "q_w1b", "q_w2a", "q_w2b",
        "g_L", "g_P", "g_X", "g_wqa", "g_wqb", "g_cca", "g_ccb", "g_recv", "g_out"]


class Prog:

    def __init__(self):
        self.recs = {e: [] for e in ENGINES}
        self.cnt = {}
        self.events = {}

    def emit(self, eng, fn, inc=None):
        self.recs[eng].append(("op", fn, inc))
        if inc is not None:
            self.cnt[inc[0]] = self.cnt.get(inc[0], 0) + inc[1]

    def wait(self, eng, sem, event):
        self.recs[eng].append(("wait", sem, event))

    def mark(self, name, sem):
        self.events[name] = self.cnt.get(sem, 0)

    def replay(self, block, semh):
        for ename in ENGINES:
            recs = self.recs[ename]
            if not recs:
                continue

            def body(eng, recs=recs):
                for r in recs:
                    if r[0] == "op":
                        inst = r[1](eng)
                        if r[2] is not None:
                            inst.then_inc(semh[r[2][0]], r[2][1])
                    else:
                        _, sem, ev = r
                        v = self.events[ev] if isinstance(ev, str) else ev
                        if v > 0:
                            eng.wait_ge(semh[sem], v)

            getattr(block, ename)(body)


def build_program(D=1024, FF=1024 * 4, NT=4096, C=512, debug=False):
    assert FF == 4 * D and C == 512
    ndt = D // 128
    nft = FF // 128
    TT = C // 128
    NC = NT // C
    G1 = 4
    NG1 = nft // G1
    NW2G = nft // 4
    nq = ndt + NW2G
    NEL1 = FF * D
    CEPS = 1e-8
    CHI = float(1.0 - 1e-8)

    nc = bass.Bass("TRN2", target_bir_lowering=False, detect_race_conditions=False)

    x_h = nc.dram_tensor("x", [NT, D], F32, kind="ExternalInput").ap()
    wfT_h = nc.dram_tensor("wfT", [D, D], F32, kind="ExternalInput").ap()
    wiT_h = nc.dram_tensor("wiT", [D, D], F32, kind="ExternalInput").ap()
    w1T_h = nc.dram_tensor("w1T", [D, FF], F32, kind="ExternalInput").ap()
    w2T_h = nc.dram_tensor("w2T", [FF, D], F32, kind="ExternalInput").ap()
    NSM = 3 * ndt + nft + 16 + 128
    sm_h = nc.dram_tensor("smalls", [128, NSM], F32, kind="ExternalInput").ap()
    idb_h = nc.dram_tensor("identb", [128, 128], BF16, kind="ExternalInput").ap()
    out_h = nc.dram_tensor("out", [NT, D], F32, kind="ExternalOutput").ap()

    kd_ = "ExternalOutput" if debug else "Internal"
    if debug:
        dbg_wabs = nc.dram_tensor("dbg_wabs", [128, ndt + NW2G], F32, kind="ExternalOutput").ap()
        dbg_gbc = nc.dram_tensor("dbg_gbc", [128, 2], F32, kind="ExternalOutput").ap()
        dbg_thr = nc.dram_tensor("dbg_thr", [128, 4], F32, kind="ExternalOutput").ap()
        dbg_vtmp = nc.dram_tensor("dbg_vtmp", [128, 6 * (C // 128)], F32, kind="ExternalOutput").ap()
        dbg_mcol = nc.dram_tensor("dbg_mcol", [128, (C // 128)], F32, kind="ExternalOutput").ap()
        dbg_sqcol = nc.dram_tensor("dbg_sqcol", [128, (C // 128)], F32, kind="ExternalOutput").ap()
    scL = nc.dram_tensor("scL", [NC, 128, ndt * C], F32, kind=kd_)
    scP = nc.dram_tensor("scP", [NC, 128, ndt * C], F32, kind=kd_)
    scX = nc.dram_tensor("scX", [NC, 128, ndt * C], BF16, kind=kd_)
    scq1 = nc.dram_tensor("scq1", [128, nft, ndt, 128], BF16, kind=kd_)
    scq2 = nc.dram_tensor("scq2", [128, ndt, nft, 128], BF16, kind=kd_)
    ccs = nc.dram_tensor("ccs", [128, 2 * ndt], F32)
    ccr = nc.dram_tensor("ccr", [8 * 128, 2 * ndt], F32)

    P = Prog()
    with ExitStack() as stack:
        e = stack.enter_context
        wf = e(nc.sbuf_tensor("wf", [128, max(ndt * D, 2 * G1 * ndt * 128)], BF16))
        wi = e(nc.sbuf_tensor("wi", [128, max(ndt * D, 2 * nft * 128)], BF16))
        xa = e(nc.sbuf_tensor("xa", [128, TT * D], F32))
        xnb = e(nc.sbuf_tensor("xnb", [128, TT * D], BF16))
        xnT = e(nc.sbuf_tensor("xnT", [128, ndt * C], BF16))
        nL = e(nc.sbuf_tensor("nL", [128, ndt * C], F32))
        Pp = e(nc.sbuf_tensor("Pp", [128, ndt * C], F32))
        hsb = e(nc.sbuf_tensor("hsb", [128, nft * C], BF16))
        gsb = e(nc.sbuf_tensor("gsb", [128, ndt * C], F32))
        gstage = e(nc.sbuf_tensor("gstage", [128, ndt * C], F32))
        fa = [e(nc.sbuf_tensor(f"fa{i}", [128, C], F32)) for i in range(2)]
        ia = [e(nc.sbuf_tensor(f"ia{i}", [128, C], F32)) for i in range(2)]
        nb = e(nc.sbuf_tensor("nb", [128, C], F32))
        ab = e(nc.sbuf_tensor("ab", [128, C], F32))
        sqb = e(nc.sbuf_tensor("sqb", [128, C], F32))
        smalls = e(nc.sbuf_tensor("smalls_sb", [128, 3 * ndt + nft + 16 + 128], F32))
        idb = e(nc.sbuf_tensor("idb", [128, 128], BF16))
        lcar = e(nc.sbuf_tensor("lcar", [128, ndt], F32))
        pcar = e(nc.sbuf_tensor("pcar", [128, ndt], F32))
        ncar2 = [e(nc.sbuf_tensor(f"ncar{i}", [128, ndt], F32)) for i in range(2)]
        ncar = ncar2[0]
        tmp2 = e(nc.sbuf_tensor("tmp2", [128, ndt], F32))
        tmpA = e(nc.sbuf_tensor("tmpA", [128, ndt], F32))
        tmpB = e(nc.sbuf_tensor("tmpB", [128, ndt], F32))
        recv = e(nc.sbuf_tensor("recv", [128, 8 * 2 * ndt], F32))
        ones = e(nc.sbuf_tensor("ones", [128, 128], F32))
        wabs = e(nc.sbuf_tensor("wabs", [128, nq], F32))
        sums = e(nc.sbuf_tensor("sums", [128, 2], F32))
        gbc = e(nc.sbuf_tensor("gbc", [128, 2], F32))
        thr = e(nc.sbuf_tensor("thr", [128, 4], F32))
        epst = e(nc.sbuf_tensor("epst", [128, 1], F32))
        mcol = e(nc.sbuf_tensor("mcol", [128, TT], F32))
        sqcol = e(nc.sbuf_tensor("sqcol", [128, TT], F32))
        vtmp = e(nc.sbuf_tensor("vtmp", [128, 6 * TT], F32))

        bfs = smalls[:, 0:ndt]
        bis = smalls[:, ndt:2 * ndt]
        b2s = smalls[:, 2 * ndt:3 * ndt]
        b1s = smalls[:, 3 * ndt:3 * ndt + nft]
        cms = smalls[:, 3 * ndt + nft:3 * ndt + nft + 8]
        cm1s = smalls[:, 3 * ndt + nft + 8:3 * ndt + nft + 16]
        idf = smalls[:, 3 * ndt + nft + 16:3 * ndt + nft + 16 + 128]

        psA = [e(nc.psum_tensor(f"psA{i}", [128, C], F32)) for i in range(4)]
        psTa = [e(nc.psum_tensor(f"psTa{i}", [128, C], BF16)) for i in range(2)]
        psTb = e(nc.psum_tensor("psTb", [128, ndt * 128], F32))

        semh = {n: e(nc.semaphore(n)) for n in SEMS}
        block = e(nc.Block())

        def col(t, j):
            return t[:, j:j + 1]

        P.emit("sync", lambda g: g.dma_start(out=smalls[:, :], in_=sm_h[:, :]), ("q_sm", 16))
        P.emit("sync", lambda g: g.dma_start(out=idb[:, :], in_=idb_h[:, :]), ("q_id", 16))
        P.emit("vector", lambda g: g.memset(ones[:, :], 1.0), None)
        P.emit("vector", lambda g: g.memset(ncar2[0][:, :], 0.0), None)
        P.emit("vector", lambda g: g.memset(epst[:, :], 1e-5), ("s_v", 1))
        P.mark("v:init", "s_v")

        def emit_w_cast(name, src_h, dst_sb, k):
            quarter = k % 4
            qsem = f"q_wc{quarter}"
            st = gstage[:, quarter * D:(quarter + 1) * D]
            if k >= 4:
                P.wait("sync", "s_sc", f"sc:wcast:{name}:{k - 4}")
            elif name == "wi" and k % 4 <= ndt - 1:
                last = (k % 4) + 4 * ((ndt - 1 - (k % 4)) // 4)
                P.wait("sync", "s_sc", f"sc:wcast:wf:{last}")
            P.emit("sync", lambda g, s=src_h, k=k, st=st:
                   g.dma_start(out=st, in_=s[k * 128:(k + 1) * 128, :]), (qsem, 16))
            P.mark(f"ld:w:{name}:{k}", qsem)
            P.wait("scalar", qsem, f"ld:w:{name}:{k}")
            P.emit("scalar", lambda g, st=st, d=dst_sb, k=k:
                   g.copy(out=d[:, k * D:(k + 1) * D], in_=st), ("s_sc", 1))
            P.mark(f"sc:wcast:{name}:{k}", "s_sc")

        def stage_of(k):
            return (gsb, "q_ga") if k % 2 == 0 else (gstage, "q_gb")

        def emit_gamma(k):
            st, qsem = stage_of(k)
            junk = hsb[:, 0:nft * 128]
            if k == 1:
                P.wait("sync", "s_sc", f"sc:wcast:wi:{ndt - 1}")
            elif k >= 2:
                P.wait("sync", "s_sc", f"sc:abs:{k - 2}")
            if k < ndt:
                P.emit("sync", lambda g, k=k, st=st:
                       g.dma_start(out=st[:, :], in_=w1T_h[k * 128:(k + 1) * 128, :]), (qsem, 16))
            else:
                gg = k - ndt
                P.emit("sync", lambda g, gg=gg, st=st:
                       g.dma_start(out=st[:, :],
                                   in_=w2T_h[gg * 512:(gg + 1) * 512, :].rearrange(
                                       "(j p) d -> p j d", p=128)), (qsem, 16))
            P.mark(f"ld:gam:{k}", qsem)
            P.wait("scalar", qsem, f"ld:gam:{k}")
            P.emit("scalar", lambda g, st=st, junk=junk, k=k:
                   g.activation(out=junk, in_=st[:, :], func=AF.Abs,
                                accum_out=col(wabs, k)), ("s_sc", 1))
            P.mark(f"sc:abs:{k}", "s_sc")

        def emit_thresh():
            P.wait("vector", "s_sc", f"sc:abs:{nq - 1}")
            P.emit("vector", lambda g: g.reduce_sum(out=col(sums, 0), in_=wabs[:, 0:ndt], axis=AX.X), None)
            P.emit("vector", lambda g: g.reduce_sum(out=col(sums, 1), in_=wabs[:, ndt:nq], axis=AX.X),
                   ("s_v", 1))
            P.mark("v:sums", "s_v")
            P.wait("tensor", "s_v", "v:sums")
            P.emit("tensor", lambda g: g.matmul(psTb[:, 0:2], ones[:, :], sums[:, :],
                                                start=True, stop=True), ("s_t", 1))
            P.mark("t:gbc", "s_t")
            P.wait("vector", "s_t", "t:gbc")
            P.emit("vector", lambda g: g.tensor_copy(out=gbc[:, :], in_=psTb[:, 0:2]), ("s_v", 1))
            P.mark("v:gbc", "s_v")
            P.wait("vector", "s_v", "v:gbc")
            P.emit("vector", lambda g: g.tensor_scalar(
                out=thr[:, 0:2], in0=gbc[:, :], scalar1=0.5 / NEL1, scalar2=0.5e-5,
                op0=ALU.mult, op1=ALU.add), ("s_v", 1))
            P.mark("v:thr2", "s_v")
            P.wait("vector", "s_v", "v:thr2")
            P.emit("vector", lambda g: g.tensor_scalar(
                out=thr[:, 2:4], in0=thr[:, 0:2], scalar1=-1.0, scalar2=None,
                op0=ALU.mult), ("s_v", 1))
            P.mark("v:thr", "s_v")
            if debug:
                P.wait("gpsimd", "s_v", "v:thr")
                P.emit("gpsimd", lambda g: g.dma_start(out=dbg_wabs, in_=wabs[:, :]), ("g_out", 16))
                P.emit("gpsimd", lambda g: g.dma_start(out=dbg_gbc, in_=gbc[:, :]), ("g_out", 16))
                P.emit("gpsimd", lambda g: g.dma_start(out=dbg_thr, in_=thr[:, :]), ("g_out", 16))
                P.wait("gpsimd", "g_out", 3 * 16)

        def emit_quant(k):
            st, qsem = stage_of(k)
            wqo = hsb[:, 0:nft * 128] if k % 2 == 0 else hsb[:, 2 * nft * 128:3 * nft * 128]
            qo = hsb[:, nft * 128:2 * nft * 128] if k % 2 == 0 else hsb[:, 3 * nft * 128:4 * nft * 128]
            gwq = "g_wqa" if k % 2 == 0 else "g_wqb"
            t = 0 if k < ndt else 1
            if k < 2:
                P.wait("sync", "s_sc", f"sc:abs:{nq - 1}")
            else:
                P.wait("sync", "s_v", f"v:stt:{k - 2}")
            if k < ndt:
                P.emit("sync", lambda g, k=k, st=st:
                       g.dma_start(out=st[:, :], in_=w1T_h[k * 128:(k + 1) * 128, :]), (qsem, 16))
            else:
                gg = k - ndt
                P.emit("sync", lambda g, gg=gg, st=st:
                       g.dma_start(out=st[:, :],
                                   in_=w2T_h[gg * 512:(gg + 1) * 512, :].rearrange(
                                       "(j p) d -> p j d", p=128)), (qsem, 16))
            P.mark(f"ld:q:{k}", qsem)
            P.wait("vector", qsem, f"ld:q:{k}")
            if k < 2:
                P.wait("vector", "s_v", "v:thr")
            if k >= 2:
                P.wait("vector", gwq, f"st:wq:{k - 2}")
            P.emit("vector", lambda g, st=st, qo=qo, t=t: g.tensor_scalar(
                out=qo, in0=st[:, :], scalar1=col(thr, t), scalar2=None,
                op0=ALU.is_gt), None)
            if k < ndt:
                P.emit("vector", lambda g, st=st, wqo=wqo, qo=qo, t=t: g.scalar_tensor_tensor(
                    out=wqo, in0=st[:, :], scalar=col(thr, 2 + t), in1=qo,
                    op0=ALU.is_lt, op1=ALU.subtract), ("s_v", 1))
            else:
                wq_view = wqo.rearrange("p (dt j q) -> p j dt q", j=4, q=128)
                for j in range(4):
                    P.emit("vector", lambda g, st=st, wq_view=wq_view, qo=qo, t=t, j=j:
                           g.scalar_tensor_tensor(
                               out=wq_view[:, j, :, :], in0=st[:, j * D:(j + 1) * D],
                               scalar=col(thr, 2 + t), in1=qo[:, j * D:(j + 1) * D],
                               op0=ALU.is_lt, op1=ALU.subtract),
                           ("s_v", 1) if j == 3 else None)
            P.mark(f"v:stt:{k}", "s_v")
            P.wait("gpsimd", "s_v", f"v:stt:{k}")
            if k < ndt:
                dst = scq1.ap()[:, :, k, :]
            else:
                gg = k - ndt
                dst = scq2.ap()[:, :, gg * 4:(gg + 1) * 4, :]
            P.emit("gpsimd", lambda g, wqo=wqo, dst=dst:
                   g.dma_start(out=dst, in_=wqo), (gwq, 16))
            P.mark(f"st:wq:{k}", gwq)

        def phaseA_chunk(c):
            if c > 0:
                P.wait("scalar", "s_v", f"v:mb:{c - 1}")
                P.emit("scalar", lambda g, c=c: g.dma_start(
                    out=xa[:, :], in_=x_h[c * C:(c + 1) * C, :].rearrange("(tt p) d -> p tt d", p=128)),
                    ("s_xa", 16))
                P.mark(f"xa:{c}", "s_xa")

            P.wait("vector", "s_xa", f"xa:{c}")
            for tt in range(TT):
                P.emit("vector", lambda g, tt=tt: g.reduce_sum(
                    out=col(mcol, tt), in_=xa[:, tt * D:(tt + 1) * D], axis=AX.X),
                    ("s_v", 1) if tt == TT - 1 else None)
            P.mark(f"v:red:{c}", "s_v")

            P.wait("scalar", "s_xa", f"xa:{c}")
            if c > 0:
                P.wait("scalar", "s_t", f"t:tp:{c - 1}")
            for tt in range(TT):
                P.emit("scalar", lambda g, tt=tt: g.activation(
                    out=xnb[:, tt * D:(tt + 1) * D], in_=xa[:, tt * D:(tt + 1) * D],
                    func=AF.Square, accum_out=col(sqcol, tt)),
                    ("s_sc", 1) if tt == TT - 1 else None)
            P.mark(f"sc:sq:{c}", "s_sc")

            P.wait("vector", "s_sc", f"sc:sq:{c}")
            P.wait("vector", "s_v", f"v:red:{c}")
            P.emit("vector", lambda g: g.tensor_scalar(
                out=vtmp[:, 0:TT], in0=mcol[:, :], scalar1=1.0 / D, scalar2=None, op0=ALU.mult),
                ("s_v", 1))
            P.mark(f"v:m:{c}", "s_v")
            P.emit("vector", lambda g: g.tensor_scalar(
                out=vtmp[:, TT:2 * TT], in0=sqcol[:, :], scalar1=1.0 / D, scalar2=None, op0=ALU.mult),
                ("s_v", 1))
            P.mark(f"v:ex2:{c}", "s_v")
            P.wait("vector", "s_v", f"v:m:{c}")
            P.emit("vector", lambda g: g.tensor_mul(
                out=vtmp[:, 2 * TT:3 * TT], in0=vtmp[:, 0:TT], in1=vtmp[:, 0:TT]), ("s_v", 1))
            P.mark(f"v:m2:{c}", "s_v")
            P.wait("vector", "s_v", f"v:m2:{c}")
            P.emit("vector", lambda g: g.scalar_tensor_tensor(
                out=vtmp[:, 3 * TT:4 * TT], in0=vtmp[:, 2 * TT:3 * TT], scalar=-1.0,
                in1=vtmp[:, TT:2 * TT], op0=ALU.mult, op1=ALU.add), ("s_v", 1))
            P.mark(f"v:var:{c}", "s_v")
            P.wait("scalar", "s_v", f"v:var:{c}")
            P.emit("scalar", lambda g: g.activation(
                out=vtmp[:, 4 * TT:5 * TT], in_=vtmp[:, 3 * TT:4 * TT], func=AF.Sqrt,
                bias=epst[:, 0:1]), ("s_sc", 1))
            P.mark(f"sc:sqrt:{c}", "s_sc")
            P.wait("vector", "s_sc", f"sc:sqrt:{c}")
            P.emit("vector", lambda g: g.reciprocal(
                out=vtmp[:, 5 * TT:6 * TT], in_=vtmp[:, 4 * TT:5 * TT]), ("s_v", 1))
            P.mark(f"v:rec:{c}", "s_v")
            P.wait("vector", "s_v", f"v:rec:{c}")
            P.emit("vector", lambda g: g.scalar_tensor_tensor(
                out=vtmp[:, 2 * TT:3 * TT], in0=vtmp[:, 0:TT], scalar=-1.0, in1=vtmp[:, 5 * TT:6 * TT],
                op0=ALU.mult, op1=ALU.mult), ("s_v", 1))
            P.mark(f"v:mb:{c}", "s_v")

            P.wait("scalar", "s_v", f"v:mb:{c}")
            for tt in range(TT):
                P.emit("scalar", lambda g, tt=tt: g.activation(
                    out=xnb[:, tt * D:(tt + 1) * D], in_=xa[:, tt * D:(tt + 1) * D],
                    func=AF.Identity, scale=col(vtmp, 5 * TT + tt), bias=col(vtmp, 2 * TT + tt)),
                    ("s_sc", 1) if tt == TT - 1 else None)
            P.mark(f"sc:xn:{c}", "s_sc")

            if c == NC - 1 and debug:
                P.wait("gpsimd", "s_sc", f"sc:xn:{c}")
                P.emit("gpsimd", lambda g: g.dma_start(out=dbg_vtmp, in_=vtmp[:, :]), ("g_out", 16))
                P.emit("gpsimd", lambda g: g.dma_start(out=dbg_mcol, in_=mcol[:, :]), ("g_out", 16))
                P.emit("gpsimd", lambda g: g.dma_start(out=dbg_sqcol, in_=sqcol[:, :]), ("g_out", 16))
                P.wait("gpsimd", "g_out", 3 * 16)

            P.wait("tensor", "s_sc", f"sc:xn:{c}")
            if c == 0:
                P.wait("tensor", "q_id", 16)
            for dt in range(ndt):
                if dt >= 2:
                    P.wait("tensor", "s_v", f"v:tpe:{c}:{dt - 2}")
                elif c > 0:
                    P.wait("tensor", "s_v", f"v:tpe:{c - 1}:{ndt - 2 + dt}")
                ps = psTa[dt % 2]
                for tt in range(TT):
                    P.emit("tensor", lambda g, tt=tt, dt=dt, ps=ps: g.transpose(
                        ps[:, tt * 128:(tt + 1) * 128],
                        xnb[:, tt * D + dt * 128: tt * D + (dt + 1) * 128], idb[:, :]),
                        ("s_t", 1) if tt == TT - 1 else None)
                P.mark(f"t:tpg:{c}:{dt}", "s_t")
                P.wait("vector", "s_t", f"t:tpg:{c}:{dt}")
                if c > 0 and dt == 0:
                    P.wait("vector", "g_X", f"st:xnT:{c - 1}")
                    P.wait("vector", "s_t", f"t:fi:{c - 1}:{ndt - 1}:i")
                P.emit("vector", lambda g, dt=dt, ps=ps: g.tensor_copy(
                    out=xnT[:, dt * C:(dt + 1) * C], in_=ps[:, :]), ("s_v", 1))
                P.mark(f"v:tpe:{c}:{dt}", "s_v")
            P.mark(f"t:tp:{c}", "s_t")

            P.wait("tensor", "s_v", f"v:tpe:{c}:{ndt - 1}")
            if c == 0:
                P.wait("tensor", "s_sc", f"sc:wcast:wi:{ndt - 1}")
                P.wait("scalar", "q_sm", 16)
            for ht in range(ndt):
                pf, pi = psA[ht % 2], psA[2 + ht % 2]
                if ht >= 2:
                    P.wait("tensor", "s_sc", f"sc:ia:{c}:{ht - 2}")
                elif c > 0:
                    P.wait("tensor", "s_sc", f"sc:ia:{c - 1}:{ndt - 2 + ht}")
                for kd in range(ndt):
                    P.emit("tensor", lambda g, kd=kd, ht=ht, pf=pf: g.matmul(
                        pf[:, :], wf[:, kd * D + ht * 128: kd * D + (ht + 1) * 128],
                        xnT[:, kd * C:(kd + 1) * C],
                        start=(kd == 0), stop=(kd == ndt - 1)),
                        ("s_t", 1) if kd == ndt - 1 else None)
                P.mark(f"t:fi:{c}:{ht}:f", "s_t")
                for kd in range(ndt):
                    P.emit("tensor", lambda g, kd=kd, ht=ht, pi=pi: g.matmul(
                        pi[:, :], wi[:, kd * D + ht * 128: kd * D + (ht + 1) * 128],
                        xnT[:, kd * C:(kd + 1) * C],
                        start=(kd == 0), stop=(kd == ndt - 1)),
                        ("s_t", 1) if kd == ndt - 1 else None)
                P.mark(f"t:fi:{c}:{ht}:i", "s_t")

                fb, ib = fa[ht % 2], ia[ht % 2]
                P.wait("scalar", "s_t", f"t:fi:{c}:{ht}:f")
                if ht >= 2:
                    P.wait("scalar", "s_v", f"v:scan:{c}:{ht - 2}")
                elif c > 0:
                    P.wait("scalar", "s_v", f"v:scan:{c - 1}:{ndt - 2 + ht}")
                P.emit("scalar", lambda g, fb=fb, ht=ht: g.activation(
                    out=fb[:, :], in_=psA[ht % 2][:, :], func=AF.Sigmoid,
                    bias=col(bfs, ht)), ("s_sc", 1))
                P.mark(f"sc:f:{c}:{ht}", "s_sc")
                P.wait("scalar", "s_t", f"t:fi:{c}:{ht}:i")
                P.emit("scalar", lambda g, ib=ib, ht=ht: g.activation(
                    out=ib[:, :], in_=psA[2 + ht % 2][:, :], func=AF.Identity,
                    bias=col(bis, ht)), ("s_sc", 1))
                P.mark(f"sc:ia:{c}:{ht}", "s_sc")

                P.wait("vector", "s_sc", f"sc:ia:{c}:{ht}")
                if ht == 0 and c > 0:
                    P.wait("vector", "g_L", f"st:L:{c - 1}")
                    P.wait("vector", "g_P", f"st:P:{c - 1}")
                P.emit("vector", lambda g, fb=fb: g.tensor_scalar(
                    out=ab[:, :], in0=fb[:, :], scalar1=CEPS, scalar2=CHI,
                    op0=ALU.max, op1=ALU.min), None)
                P.emit("vector", lambda g, ib=ib: g.scalar_tensor_tensor(
                    out=nb[:, :], in0=ab[:, :], scalar=1.0, in1=ib[:, :],
                    op0=ALU.subtract, op1=ALU.mult), None)
                initL = 0.0 if c == 0 else col(lcar, ht)
                initP = 1.0 if c == 0 else col(pcar, ht)
                P.emit("vector", lambda g, ht=ht, initL=initL: g.tensor_tensor_scan(
                    out=nL[:, ht * C:(ht + 1) * C], data0=ab[:, :], data1=nb[:, :],
                    initial=initL, op0=ALU.mult, op1=ALU.add), None)
                P.emit("vector", lambda g, ht=ht, initP=initP: g.tensor_tensor_scan(
                    out=Pp[:, ht * C:(ht + 1) * C], data0=ab[:, :], data1=ab[:, :],
                    initial=initP, op0=ALU.mult, op1=ALU.min), ("s_v", 1))
                P.mark(f"v:scan:{c}:{ht}", "s_v")

            P.emit("vector", lambda g: g.tensor_copy(
                out=lcar[:, :], in_=nL.ap().rearrange("p (h t) -> p h t", t=C)[:, :, C - 1]), None)
            P.emit("vector", lambda g: g.tensor_copy(
                out=pcar[:, :], in_=Pp.ap().rearrange("p (h t) -> p h t", t=C)[:, :, C - 1]),
                ("s_v", 1))
            P.mark(f"v:car:{c}", "s_v")

            P.wait("gpsimd", "s_v", f"v:car:{c}")
            P.emit("gpsimd", lambda g, c=c: g.dma_start(out=scL.ap()[c], in_=nL[:, :]), ("g_L", 16))
            P.mark(f"st:L:{c}", "g_L")
            P.emit("gpsimd", lambda g, c=c: g.dma_start(out=scP.ap()[c], in_=Pp[:, :]), ("g_P", 16))
            P.mark(f"st:P:{c}", "g_P")
            P.emit("gpsimd", lambda g, c=c: g.dma_start(out=scX.ap()[c], in_=xnT[:, :]), ("g_X", 16))
            P.mark(f"st:xnT:{c}", "g_X")

        def emit_cc():
            P.emit("gpsimd", lambda g: g.dma_start(out=ccs.ap()[:, 0:ndt], in_=lcar[:, :]), ("g_cca", 16))
            P.emit("gpsimd", lambda g: g.dma_start(out=ccs.ap()[:, ndt:2 * ndt], in_=pcar[:, :]), ("g_ccb", 16))
            P.wait("gpsimd", "g_cca", 16)
            P.wait("gpsimd", "g_ccb", 16)
            P.emit("gpsimd", lambda g: g.collective_compute(
                "AllGather", ALU.bypass, replica_groups=[list(range(8))],
                ins=[ccs.ap().opt()], outs=[ccr.ap().opt()]), ("s_cc", 1))
            P.wait("gpsimd", "s_cc", 1)
            P.emit("gpsimd", lambda g: g.dma_start(
                out=recv[:, :],
                in_=ccr.ap().rearrange("(r p) c -> p r c", p=128)), ("g_recv", 16))
            P.wait("vector", "g_recv", 16)
            P.wait("vector", "q_sm", 16)
            for j in range(8):
                Aj = recv[:, j * 2 * ndt + ndt:(j + 1) * 2 * ndt]
                Bj = recv[:, j * 2 * ndt: j * 2 * ndt + ndt]
                P.emit("vector", lambda g, Aj=Aj, j=j: g.tensor_scalar(
                    out=tmpA[:, :], in0=Aj, scalar1=col(cms, j), scalar2=col(cm1s, j),
                    op0=ALU.mult, op1=ALU.add), None)
                P.emit("vector", lambda g, Bj=Bj, j=j: g.tensor_scalar(
                    out=tmpB[:, :], in0=Bj, scalar1=col(cms, j), scalar2=None,
                    op0=ALU.mult), ("s_v", 1))
                P.mark(f"v:cab:{j}", "s_v")
                P.wait("vector", "s_v", f"v:cab:{j}")
                P.emit("vector", lambda g, j=j: g.tensor_mul(
                    out=tmp2[:, :], in0=ncar2[j % 2][:, :], in1=tmpA[:, :]), ("s_v", 1))
                P.mark(f"v:cm:{j}", "s_v")
                P.wait("vector", "s_v", f"v:cm:{j}")
                P.emit("vector", lambda g, j=j: g.tensor_add(
                    out=ncar2[(j + 1) % 2][:, :], in0=tmp2[:, :], in1=tmpB[:, :]), ("s_v", 1))
                P.mark(f"v:ca:{j}", "s_v")
                P.wait("vector", "s_v", f"v:ca:{j}")
            P.mark("v:ncar", "s_v")

        def phaseB_chunk(c):
            P.wait("sync", "g_X", f"st:xnT:{c}")
            if c == 0:
                P.wait("sync", "g_X", f"st:xnT:{NC - 1}")
                P.wait("sync", "g_L", f"st:L:{NC - 1}")
                P.wait("sync", "g_P", f"st:P:{NC - 1}")
                P.wait("sync", "g_cca", 16)
                P.wait("sync", "g_ccb", 16)
                P.wait("sync", "s_t", f"t:fi:{NC - 1}:{ndt - 1}:i")
            else:
                P.wait("sync", "s_t", f"t:hg:{c - 1}:{nft - 1}")
            P.emit("sync", lambda g, c=c: g.dma_start(out=xnT[:, :], in_=scX.ap()[c]), ("q_xnT", 16))
            P.mark(f"ld:BxnT:{c}", "q_xnT")
            if c > 0:
                P.wait("sync", "s_t", f"t:qt:{c - 1}:{TT - 1}")
                P.wait("sync", "s_v", f"v:sq:{c - 1}")
            P.wait("sync", "g_L", f"st:L:{c}")
            P.emit("sync", lambda g, c=c: g.dma_start(out=nL[:, :], in_=scL.ap()[c]), ("q_nL", 16))
            P.mark(f"ld:BnL:{c}", "q_nL")
            P.wait("sync", "g_P", f"st:P:{c}")
            P.emit("sync", lambda g, c=c: g.dma_start(out=Pp[:, :], in_=scP.ap()[c]), ("q_Pp", 16))
            P.mark(f"ld:BPp:{c}", "q_Pp")
            if c == 0:
                P.wait("sync", "s_sc", f"sc:xn:{NC - 1}")
                P.wait("sync", "s_v", f"v:mb:{NC - 1}")
            else:
                P.wait("sync", "s_v", f"v:cmb:{c - 1}:{TT - 1}")
            P.emit("sync", lambda g, c=c: g.dma_start(
                out=xa[:, :], in_=x_h[c * C:(c + 1) * C, :].rearrange("(tt p) d -> p tt d", p=128)),
                ("q_xa", 16))
            P.mark(f"ld:Bxa:{c}", "q_xa")
            for g1 in range(NG1):
                w1sem = "q_w1a" if g1 % 2 == 0 else "q_w1b"
                if c == 0 and g1 < 2:
                    P.wait("sync", "s_t", f"t:fi:{NC - 1}:{ndt - 1}:i")
                    P.wait("sync", "g_wqa", f"st:wq:{nq - 2}")
                    P.wait("sync", "g_wqb", f"st:wq:{nq - 1}")
                elif g1 >= 2:
                    P.wait("sync", "s_t", f"t:hg:{c}:{(g1 - 2) * G1 + G1 - 1}")
                else:
                    pg = g1 + 2 * ((NG1 - 1 - g1) // 2)
                    P.wait("sync", "s_t", f"t:hg:{c - 1}:{pg * G1 + G1 - 1}")
                P.emit("sync", lambda g, g1=g1: g.dma_start(
                    out=wf[:, (g1 % 2) * (G1 * ndt * 128):(g1 % 2 + 1) * (G1 * ndt * 128)],
                    in_=scq1.ap()[:, g1 * G1:(g1 + 1) * G1, :, :]), (w1sem, 16))
                P.mark(f"ld:wq1:{c}:{g1}", w1sem)
            for dt in range(ndt):
                w2sem = "q_w2a" if dt % 2 == 0 else "q_w2b"
                if c == 0 and dt < 2:
                    P.wait("sync", "s_t", f"t:fi:{NC - 1}:{ndt - 1}:i")
                    P.wait("sync", "g_wqa", f"st:wq:{nq - 2}")
                    P.wait("sync", "g_wqb", f"st:wq:{nq - 1}")
                elif dt >= 2:
                    P.wait("sync", "s_t", f"t:o:{c}:{dt - 2}")
                else:
                    P.wait("sync", "s_t", f"t:o:{c - 1}:{ndt - 2 + dt}")
                P.emit("sync", lambda g, dt=dt: g.dma_start(
                    out=wi[:, (dt % 2) * (nft * 128):(dt % 2 + 1) * (nft * 128)],
                    in_=scq2.ap()[:, dt, :, :]), (w2sem, 16))
                P.mark(f"ld:wq2:{c}:{dt}", w2sem)

            P.wait("tensor", "q_xnT", f"ld:BxnT:{c}")
            for m in range(nft):
                ph = psA[m % 2]
                P.wait("tensor", "q_w1a" if (m // G1) % 2 == 0 else "q_w1b", f"ld:wq1:{c}:{m // G1}")
                if m >= 2:
                    P.wait("tensor", "s_sc", f"sc:h:{c}:{m - 2}")
                elif c > 0:
                    P.wait("tensor", "s_sc", f"sc:h:{c - 1}:{nft - 2 + m}")
                else:
                    P.wait("tensor", "s_sc", f"sc:ia:{NC - 1}:{ndt - 2 + m}")
                base = ((m // G1) % 2) * (G1 * ndt * 128) + (m % G1) * ndt * 128
                for kd in range(ndt):
                    P.emit("tensor", lambda g, kd=kd, base=base, ph=ph: g.matmul(
                        ph[:, :], wf[:, base + kd * 128: base + (kd + 1) * 128],
                        xnT[:, kd * C:(kd + 1) * C],
                        start=(kd == 0), stop=(kd == ndt - 1)),
                        ("s_t", 1) if kd == ndt - 1 else None)
                P.mark(f"t:hg:{c}:{m}", "s_t")
                P.wait("scalar", "s_t", f"t:hg:{c}:{m}")
                if c == 0 and m == 0:
                    P.wait("scalar", "g_wqa", f"st:wq:{nq - 2}")
                    P.wait("scalar", "g_wqb", f"st:wq:{nq - 1}")
                    P.wait("scalar", "s_v", f"v:stt:{nq - 1}")
                P.emit("scalar", lambda g, m=m, ph=ph: g.activation(
                    out=hsb[:, m * C:(m + 1) * C], in_=ph[:, :], func=AF.Relu,
                    scale=-1.0, bias=col(b1s, m)), ("s_sc", 1))
                P.mark(f"sc:h:{c}:{m}", "s_sc")

            for dt in range(ndt):
                po = psA[2 + dt % 2]
                P.wait("tensor", "q_w2a" if dt % 2 == 0 else "q_w2b", f"ld:wq2:{c}:{dt}")
                if dt == 0:
                    P.wait("tensor", "s_sc", f"sc:h:{c}:{nft - 1}")
                if dt >= 2:
                    P.wait("tensor", "s_sc", f"sc:g:{c}:{dt - 2}")
                elif c > 0:
                    P.wait("tensor", "s_sc", f"sc:g:{c - 1}:{ndt - 2 + dt}")
                else:
                    P.wait("tensor", "s_sc", f"sc:ia:{NC - 1}:{ndt - 2 + dt}")
                wbase = (dt % 2) * (nft * 128)
                for kf in range(nft):
                    P.emit("tensor", lambda g, kf=kf, wbase=wbase, po=po: g.matmul(
                        po[:, :], wi[:, wbase + kf * 128: wbase + (kf + 1) * 128],
                        hsb[:, kf * C:(kf + 1) * C],
                        start=(kf == 0), stop=(kf == nft - 1)),
                        ("s_t", 1) if kf == nft - 1 else None)
                P.mark(f"t:o:{c}:{dt}", "s_t")
                P.wait("scalar", "s_t", f"t:o:{c}:{dt}")
                if c == 0 and dt == 0:
                    P.wait("scalar", "s_v", f"v:stt:{nq - 1}")
                if dt >= 1 or c > 0:
                    prev = (c, dt - 1) if dt >= 1 else (c - 1, ndt - 1)
                    P.wait("scalar", "s_v", f"v:qg:{prev[0]}:{prev[1]}")
                if c > 0 and dt == 0:
                    P.wait("scalar", "g_out", f"st:out:{c - 1}")
                    P.wait("scalar", "s_v", f"v:cmb:{c - 1}:{TT - 1}")
                P.emit("scalar", lambda g, dt=dt, po=po: g.activation(
                    out=gsb[:, dt * C:(dt + 1) * C], in_=po[:, :], func=AF.Silu,
                    scale=-1.0, bias=col(b2s, dt)), ("s_sc", 1))
                P.mark(f"sc:g:{c}:{dt}", "s_sc")

                P.wait("vector", "q_nL", f"ld:BnL:{c}")
                P.wait("vector", "q_Pp", f"ld:BPp:{c}")
                if c == 0 and dt == 0:
                    P.wait("vector", "s_v", "v:ncar")
                P.emit("vector", lambda g, dt=dt: g.scalar_tensor_tensor(
                    out=sqb[:, :], in0=Pp[:, dt * C:(dt + 1) * C], scalar=col(ncar, dt),
                    in1=nL[:, dt * C:(dt + 1) * C], op0=ALU.mult, op1=ALU.add),
                    ("s_v", 1) if dt == ndt - 1 else None)
                if dt == ndt - 1:
                    P.mark(f"v:sq:{c}", "s_v")
                P.wait("vector", "s_sc", f"sc:g:{c}:{dt}")
                P.emit("vector", lambda g, dt=dt: g.tensor_mul(
                    out=nL[:, dt * C:(dt + 1) * C], in0=gsb[:, dt * C:(dt + 1) * C],
                    in1=sqb[:, :]), ("s_v", 1))
                P.mark(f"v:qg:{c}:{dt}", "s_v")

            for tt in range(TT):
                P.wait("tensor", "s_v", f"v:qg:{c}:{ndt - 1}")
                if tt > 0:
                    P.wait("tensor", "s_v", f"v:cmb:{c}:{tt - 1}")
                elif c > 0:
                    P.wait("tensor", "s_v", f"v:cmb:{c - 1}:{TT - 1}")
                else:
                    P.wait("tensor", "s_v", "v:thr")
                for ht in range(ndt):
                    P.emit("tensor", lambda g, tt=tt, ht=ht: g.transpose(
                        psTb[:, ht * 128:(ht + 1) * 128],
                        nL[:, ht * C + tt * 128: ht * C + (tt + 1) * 128], idf),
                        ("s_t", 1) if ht == ndt - 1 else None)
                P.mark(f"t:qt:{c}:{tt}", "s_t")
                P.wait("vector", "s_t", f"t:qt:{c}:{tt}")
                if tt == 0:
                    P.wait("vector", "q_xa", f"ld:Bxa:{c}")
                P.emit("vector", lambda g, tt=tt: g.tensor_sub(
                    out=gsb[:, tt * D:(tt + 1) * D], in0=xa[:, tt * D:(tt + 1) * D],
                    in1=psTb[:, 0:D]), ("s_v", 1))
                P.mark(f"v:cmb:{c}:{tt}", "s_v")

            P.wait("gpsimd", "s_v", f"v:cmb:{c}:{TT - 1}")
            P.emit("gpsimd", lambda g, c=c: g.dma_start(
                out=out_h[c * C:(c + 1) * C, :].rearrange("(tt p) d -> p tt d", p=128),
                in_=gsb[:, :]), ("g_out", 16))
            P.mark(f"st:out:{c}", "g_out")

        P.emit("scalar", lambda g: g.dma_start(
            out=xa[:, :], in_=x_h[0:C, :].rearrange("(tt p) d -> p tt d", p=128)),
            ("s_xa", 16))
        P.mark("xa:0", "s_xa")
        for k in range(ndt):
            emit_w_cast("wf", wfT_h, wf, k)
        for k in range(ndt):
            emit_w_cast("wi", wiT_h, wi, k)
        prologue = ([("g", k) for k in range(nq)] + [("t", 0)]
                    + [("q", k) for k in range(nq)])
        per_slot = -(-len(prologue) // min(NC, 6))
        for c in range(NC):
            phaseA_chunk(c)
            for item in prologue[c * per_slot:(c + 1) * per_slot]:
                if item[0] == "g":
                    emit_gamma(item[1])
                elif item[0] == "t":
                    emit_thresh()
                else:
                    emit_quant(item[1])
        for item in prologue[NC * per_slot:]:
            if item[0] == "g":
                emit_gamma(item[1])
            elif item[0] == "t":
                emit_thresh()
            else:
                emit_quant(item[1])
        emit_cc()
        for c in range(NC):
            phaseB_chunk(c)

        P.recs["gpsimd"].append(("wait", "g_out", P.cnt.get("g_out", 0)))

        P.replay(block, semh)

    return nc


_NC_CACHE = {}


def _get_nc():
    if "full" not in _NC_CACHE:
        _NC_CACHE["full"] = build_program()
    return _NC_CACHE["full"]


def kernel(L0_attn_x, L0_attn_W_f, L0_attn_b_f, L0_attn_W_i, L0_attn_b_i,
           L0_ffn_W1, L0_ffn_b1, L0_ffn_W2, L0_ffn_b2):
    D, FF, NT = 1024, 4096, 4096
    ndt, nft = D // 128, FF // 128
    x = np.ascontiguousarray(np.asarray(L0_attn_x, dtype=np.float32))
    wfT = np.ascontiguousarray(np.asarray(L0_attn_W_f, dtype=np.float32).T)
    wiT = np.ascontiguousarray(np.asarray(L0_attn_W_i, dtype=np.float32).T)
    w1T = np.ascontiguousarray(np.asarray(L0_ffn_W1, dtype=np.float32).T)
    w2T = np.ascontiguousarray(np.asarray(L0_ffn_W2, dtype=np.float32).T)
    bfv = np.asarray(L0_attn_b_f, dtype=np.float32).reshape(ndt, 128).T
    biv = np.asarray(L0_attn_b_i, dtype=np.float32).reshape(ndt, 128).T
    b1v = np.asarray(L0_ffn_b1, dtype=np.float32).reshape(nft, 128).T
    b2v = np.asarray(L0_ffn_b2, dtype=np.float32).reshape(ndt, 128).T
    identb = np.eye(128, dtype=ml_dtypes.bfloat16)
    identf = np.eye(128, dtype=np.float32)

    nc = _get_nc()
    in_maps = []
    for k in range(8):
        b, s = k // 4, k % 4
        mask = np.array([1.0 if (j // 4 == k // 4 and j < k) else 0.0 for j in range(8)],
                        dtype=np.float32)
        sm = np.concatenate([
            bfv, biv, b2v, b1v,
            np.tile(mask, (128, 1)), np.tile(1.0 - mask, (128, 1)), identf,
        ], axis=1).astype(np.float32)
        in_maps.append({
            "x": np.ascontiguousarray(x[b, s * NT:(s + 1) * NT, :]),
            "wfT": wfT, "wiT": wiT, "w1T": w1T, "w2T": w2T,
            "smalls": np.ascontiguousarray(sm), "identb": identb,
        })
    res = run_bass_kernel_spmd(nc, in_maps, core_ids=list(range(8)))
    out = np.empty((2, 4 * NT, D), dtype=np.float32)
    for k in range(8):
        b, s = k // 4, k % 4
        out[b, s * NT:(s + 1) * NT, :] = res.results[k]["out"]
    return out


# version 05
# speedup vs baseline: 1.5663x; 1.1127x over previous
import numpy as np
import ml_dtypes
from contextlib import ExitStack

import concourse.bass as bass
import concourse.mybir as mybir
from concourse.bass_utils import run_bass_kernel_spmd

F32 = mybir.dt.float32
BF16 = mybir.dt.bfloat16
AF = mybir.ActivationFunctionType
ALU = mybir.AluOpType
AX = mybir.AxisListType

ENGINES = ("sync", "scalar", "vector", "tensor", "gpsimd")

SEMS = ["s_t", "s_v", "s_sc", "s_g", "s_cc", "s_xa",
        "q_sm", "q_id", "q_wc0", "q_wc1", "q_wc2", "q_wc3", "q_ga", "q_gb",
        "q_xnT", "q_nL", "q_Pp", "q_xa", "q_w1a", "q_w1b", "q_w2a", "q_w2b",
        "g_L", "g_P", "g_X", "g_wqa", "g_wqb", "g_cca", "g_ccb", "g_recv", "g_out"]


class Prog:

    def __init__(self):
        self.recs = {e: [] for e in ENGINES}
        self.cnt = {}
        self.events = {}

    def emit(self, eng, fn, inc=None):
        self.recs[eng].append(("op", fn, inc))
        if inc is not None:
            self.cnt[inc[0]] = self.cnt.get(inc[0], 0) + inc[1]

    def wait(self, eng, sem, event):
        self.recs[eng].append(("wait", sem, event))

    def mark(self, name, sem):
        self.events[name] = self.cnt.get(sem, 0)

    def replay(self, block, semh):
        for ename in ENGINES:
            recs = self.recs[ename]
            if not recs:
                continue

            def body(eng, recs=recs):
                for r in recs:
                    if r[0] == "op":
                        inst = r[1](eng)
                        if r[2] is not None:
                            inst.then_inc(semh[r[2][0]], r[2][1])
                    else:
                        _, sem, ev = r
                        v = self.events[ev] if isinstance(ev, str) else ev
                        if v > 0:
                            eng.wait_ge(semh[sem], v)

            getattr(block, ename)(body)


def build_program(D=1024, FF=1024 * 4, NT=4096, C=512, debug=False):
    assert FF == 4 * D and C == 512
    ndt = D // 128
    nft = FF // 128
    TT = C // 128
    NC = NT // C
    G1 = 4
    NG1 = nft // G1
    NW2G = nft // 4
    nq = ndt + NW2G
    NEL1 = FF * D
    CEPS = 1e-8
    CHI = float(1.0 - 1e-8)

    nc = bass.Bass("TRN2", target_bir_lowering=False, detect_race_conditions=False)

    x_h = nc.dram_tensor("x", [NT, D], F32, kind="ExternalInput").ap()
    wfT_h = nc.dram_tensor("wfT", [D, D], F32, kind="ExternalInput").ap()
    wiT_h = nc.dram_tensor("wiT", [D, D], F32, kind="ExternalInput").ap()
    w1T_h = nc.dram_tensor("w1T", [D, FF], F32, kind="ExternalInput").ap()
    w2T_h = nc.dram_tensor("w2T", [FF, D], F32, kind="ExternalInput").ap()
    NSM = 3 * ndt + nft + 16 + 128
    sm_h = nc.dram_tensor("smalls", [128, NSM], F32, kind="ExternalInput").ap()
    idb_h = nc.dram_tensor("identb", [128, 128], BF16, kind="ExternalInput").ap()
    out_h = nc.dram_tensor("out", [NT, D], F32, kind="ExternalOutput").ap()

    kd_ = "ExternalOutput" if debug else "Internal"
    if debug:
        dbg_wabs = nc.dram_tensor("dbg_wabs", [128, ndt + NW2G], F32, kind="ExternalOutput").ap()
        dbg_gbc = nc.dram_tensor("dbg_gbc", [128, 2], F32, kind="ExternalOutput").ap()
        dbg_thr = nc.dram_tensor("dbg_thr", [128, 4], F32, kind="ExternalOutput").ap()
        dbg_vtmp = nc.dram_tensor("dbg_vtmp", [128, 6 * (C // 128)], F32, kind="ExternalOutput").ap()
        dbg_mcol = nc.dram_tensor("dbg_mcol", [128, (C // 128)], F32, kind="ExternalOutput").ap()
        dbg_sqcol = nc.dram_tensor("dbg_sqcol", [128, (C // 128)], F32, kind="ExternalOutput").ap()
    scL = nc.dram_tensor("scL", [NC, 128, ndt * C], F32, kind=kd_)
    scP = nc.dram_tensor("scP", [NC, 128, ndt * C], F32, kind=kd_)
    scX = nc.dram_tensor("scX", [NC, 128, ndt * C], BF16, kind=kd_)
    scq1 = nc.dram_tensor("scq1", [128, nft, ndt, 128], BF16, kind=kd_)
    scq2 = nc.dram_tensor("scq2", [128, ndt, nft, 128], BF16, kind=kd_)
    ccs = nc.dram_tensor("ccs", [128, 2 * ndt], F32)
    ccr = nc.dram_tensor("ccr", [8 * 128, 2 * ndt], F32)

    P = Prog()
    with ExitStack() as stack:
        e = stack.enter_context
        wf = e(nc.sbuf_tensor("wf", [128, max(ndt * D, 2 * G1 * ndt * 128)], BF16))
        wi = e(nc.sbuf_tensor("wi", [128, max(ndt * D, 2 * nft * 128)], BF16))
        xa = e(nc.sbuf_tensor("xa", [128, TT * D], F32))
        xnb = e(nc.sbuf_tensor("xnb", [128, TT * D], BF16))
        xnT = e(nc.sbuf_tensor("xnT", [128, ndt * C], BF16))
        nL = e(nc.sbuf_tensor("nL", [128, ndt * C], F32))
        Pp = e(nc.sbuf_tensor("Pp", [128, ndt * C], F32))
        hsb = e(nc.sbuf_tensor("hsb", [128, nft * C], BF16))
        gsb = e(nc.sbuf_tensor("gsb", [128, ndt * C], F32))
        gstage = e(nc.sbuf_tensor("gstage", [128, ndt * C], F32))
        fa = [e(nc.sbuf_tensor(f"fa{i}", [128, C], F32)) for i in range(2)]
        ia = [e(nc.sbuf_tensor(f"ia{i}", [128, C], F32)) for i in range(2)]
        nb = e(nc.sbuf_tensor("nb", [128, C], F32))
        ab = e(nc.sbuf_tensor("ab", [128, C], F32))
        sqb = e(nc.sbuf_tensor("sqb", [128, C], F32))
        smalls = e(nc.sbuf_tensor("smalls_sb", [128, 3 * ndt + nft + 16 + 128], F32))
        idb = e(nc.sbuf_tensor("idb", [128, 128], BF16))
        lcar = e(nc.sbuf_tensor("lcar", [128, ndt], F32))
        pcar = e(nc.sbuf_tensor("pcar", [128, ndt], F32))
        ncar2 = [e(nc.sbuf_tensor(f"ncar{i}", [128, ndt], F32)) for i in range(2)]
        ncar = ncar2[0]
        tmp2 = e(nc.sbuf_tensor("tmp2", [128, ndt], F32))
        tmpA = e(nc.sbuf_tensor("tmpA", [128, ndt], F32))
        tmpB = e(nc.sbuf_tensor("tmpB", [128, ndt], F32))
        recv = e(nc.sbuf_tensor("recv", [128, 8 * 2 * ndt], F32))
        ones = e(nc.sbuf_tensor("ones", [128, 128], F32))
        wabs = e(nc.sbuf_tensor("wabs", [128, nq], F32))
        sums = e(nc.sbuf_tensor("sums", [128, 2], F32))
        gbc = e(nc.sbuf_tensor("gbc", [128, 2], F32))
        thr = e(nc.sbuf_tensor("thr", [128, 4], F32))
        epst = e(nc.sbuf_tensor("epst", [128, 1], F32))
        mcol = e(nc.sbuf_tensor("mcol", [128, TT], F32))
        sqcol = e(nc.sbuf_tensor("sqcol", [128, TT], F32))
        vtmp = e(nc.sbuf_tensor("vtmp", [128, 6 * TT], F32))

        bfs = smalls[:, 0:ndt]
        bis = smalls[:, ndt:2 * ndt]
        b2s = smalls[:, 2 * ndt:3 * ndt]
        b1s = smalls[:, 3 * ndt:3 * ndt + nft]
        cms = smalls[:, 3 * ndt + nft:3 * ndt + nft + 8]
        cm1s = smalls[:, 3 * ndt + nft + 8:3 * ndt + nft + 16]
        idf = smalls[:, 3 * ndt + nft + 16:3 * ndt + nft + 16 + 128]

        psA = [e(nc.psum_tensor(f"psA{i}", [128, C], F32)) for i in range(4)]
        psTa = [e(nc.psum_tensor(f"psTa{i}", [128, C], BF16)) for i in range(2)]
        psTb2 = [e(nc.psum_tensor(f"psTb{i}", [128, ndt * 128], BF16)) for i in range(2)]

        semh = {n: e(nc.semaphore(n)) for n in SEMS}
        block = e(nc.Block())

        def col(t, j):
            return t[:, j:j + 1]

        P.emit("sync", lambda g: g.dma_start(out=smalls[:, :], in_=sm_h[:, :]), ("q_sm", 16))
        P.emit("sync", lambda g: g.dma_start(out=idb[:, :], in_=idb_h[:, :]), ("q_id", 16))
        P.emit("vector", lambda g: g.memset(ones[:, :], 1.0), None)
        P.emit("vector", lambda g: g.memset(ncar2[0][:, :], 0.0), None)
        P.emit("vector", lambda g: g.memset(epst[:, :], 1e-5), ("s_v", 1))
        P.mark("v:init", "s_v")

        def emit_w_cast(name, src_h, dst_sb, k):
            quarter = k % 4
            qsem = f"q_wc{quarter}"
            st = gstage[:, quarter * D:(quarter + 1) * D]
            if k >= 4:
                P.wait("sync", "s_sc", f"sc:wcast:{name}:{k - 4}")
            elif name == "wi" and k % 4 <= ndt - 1:
                last = (k % 4) + 4 * ((ndt - 1 - (k % 4)) // 4)
                P.wait("sync", "s_sc", f"sc:wcast:wf:{last}")
            P.emit("sync", lambda g, s=src_h, k=k, st=st:
                   g.dma_start(out=st, in_=s[k * 128:(k + 1) * 128, :]), (qsem, 16))
            P.mark(f"ld:w:{name}:{k}", qsem)
            P.wait("scalar", qsem, f"ld:w:{name}:{k}")
            P.emit("scalar", lambda g, st=st, d=dst_sb, k=k:
                   g.copy(out=d[:, k * D:(k + 1) * D], in_=st), ("s_sc", 1))
            P.mark(f"sc:wcast:{name}:{k}", "s_sc")

        def stage_of(k):
            return (gsb, "q_ga") if k % 2 == 0 else (gstage, "q_gb")

        def emit_gamma(k):
            st, qsem = stage_of(k)
            junk = hsb[:, 0:nft * 128]
            if k == 1:
                P.wait("sync", "s_sc", f"sc:wcast:wi:{ndt - 1}")
            elif k >= 2:
                P.wait("sync", "s_sc", f"sc:abs:{k - 2}")
            if k < ndt:
                P.emit("sync", lambda g, k=k, st=st:
                       g.dma_start(out=st[:, :], in_=w1T_h[k * 128:(k + 1) * 128, :]), (qsem, 16))
            else:
                gg = k - ndt
                P.emit("sync", lambda g, gg=gg, st=st:
                       g.dma_start(out=st[:, :],
                                   in_=w2T_h[gg * 512:(gg + 1) * 512, :].rearrange(
                                       "(j p) d -> p j d", p=128)), (qsem, 16))
            P.mark(f"ld:gam:{k}", qsem)
            P.wait("scalar", qsem, f"ld:gam:{k}")
            P.emit("scalar", lambda g, st=st, junk=junk, k=k:
                   g.activation(out=junk, in_=st[:, :], func=AF.Abs,
                                accum_out=col(wabs, k)), ("s_sc", 1))
            P.mark(f"sc:abs:{k}", "s_sc")

        def emit_thresh(cslot):
            P.wait("vector", "s_sc", f"sc:abs:{nq - 1}")
            P.emit("vector", lambda g: g.reduce_sum(out=col(sums, 0), in_=wabs[:, 0:ndt], axis=AX.X), None)
            P.emit("vector", lambda g: g.reduce_sum(out=col(sums, 1), in_=wabs[:, ndt:nq], axis=AX.X),
                   ("s_v", 1))
            P.mark("v:sums", "s_v")
            P.wait("tensor", "s_v", "v:sums")
            P.wait("tensor", "s_sc", f"sc:ia:{cslot}:{ndt - 1}")
            P.emit("tensor", lambda g: g.matmul(psA[3][:, 0:2], ones[:, :], sums[:, :],
                                                start=True, stop=True), ("s_t", 1))
            P.mark("t:gbc", "s_t")
            P.wait("vector", "s_t", "t:gbc")
            P.emit("vector", lambda g: g.tensor_copy(out=gbc[:, :], in_=psA[3][:, 0:2]), ("s_v", 1))
            P.mark("v:gbce", "s_v")
            P.wait("tensor", "s_v", "v:gbce")
            P.mark("v:gbc", "s_v")
            P.wait("vector", "s_v", "v:gbc")
            P.emit("vector", lambda g: g.tensor_scalar(
                out=thr[:, 0:2], in0=gbc[:, :], scalar1=0.5 / NEL1, scalar2=0.5e-5,
                op0=ALU.mult, op1=ALU.add), ("s_v", 1))
            P.mark("v:thr2", "s_v")
            P.wait("vector", "s_v", "v:thr2")
            P.emit("vector", lambda g: g.tensor_scalar(
                out=thr[:, 2:4], in0=thr[:, 0:2], scalar1=-1.0, scalar2=None,
                op0=ALU.mult), ("s_v", 1))
            P.mark("v:thr", "s_v")
            if debug:
                P.wait("gpsimd", "s_v", "v:thr")
                P.emit("gpsimd", lambda g: g.dma_start(out=dbg_wabs, in_=wabs[:, :]), ("g_out", 16))
                P.emit("gpsimd", lambda g: g.dma_start(out=dbg_gbc, in_=gbc[:, :]), ("g_out", 16))
                P.emit("gpsimd", lambda g: g.dma_start(out=dbg_thr, in_=thr[:, :]), ("g_out", 16))
                P.wait("gpsimd", "g_out", 3 * 16)

        def emit_quant(k):
            st, qsem = stage_of(k)
            wqo = hsb[:, 0:nft * 128] if k % 2 == 0 else hsb[:, 2 * nft * 128:3 * nft * 128]
            qo = hsb[:, nft * 128:2 * nft * 128] if k % 2 == 0 else hsb[:, 3 * nft * 128:4 * nft * 128]
            gwq = "g_wqa" if k % 2 == 0 else "g_wqb"
            t = 0 if k < ndt else 1
            if k < 2:
                P.wait("sync", "s_sc", f"sc:abs:{nq - 1}")
            else:
                P.wait("sync", "s_v", f"v:stt:{k - 2}")
            if k < ndt:
                P.emit("sync", lambda g, k=k, st=st:
                       g.dma_start(out=st[:, :], in_=w1T_h[k * 128:(k + 1) * 128, :]), (qsem, 16))
            else:
                gg = k - ndt
                P.emit("sync", lambda g, gg=gg, st=st:
                       g.dma_start(out=st[:, :],
                                   in_=w2T_h[gg * 512:(gg + 1) * 512, :].rearrange(
                                       "(j p) d -> p j d", p=128)), (qsem, 16))
            P.mark(f"ld:q:{k}", qsem)
            P.wait("vector", qsem, f"ld:q:{k}")
            if k < 2:
                P.wait("vector", "s_v", "v:thr")
            if k >= 2:
                P.wait("vector", gwq, f"st:wq:{k - 2}")
            P.emit("vector", lambda g, st=st, qo=qo, t=t: g.tensor_scalar(
                out=qo, in0=st[:, :], scalar1=col(thr, t), scalar2=None,
                op0=ALU.is_gt), None)
            if k < ndt:
                P.emit("vector", lambda g, st=st, wqo=wqo, qo=qo, t=t: g.scalar_tensor_tensor(
                    out=wqo, in0=st[:, :], scalar=col(thr, 2 + t), in1=qo,
                    op0=ALU.is_lt, op1=ALU.subtract), ("s_v", 1))
            else:
                wq_view = wqo.rearrange("p (dt j q) -> p j dt q", j=4, q=128)
                for j in range(4):
                    P.emit("vector", lambda g, st=st, wq_view=wq_view, qo=qo, t=t, j=j:
                           g.scalar_tensor_tensor(
                               out=wq_view[:, j, :, :], in0=st[:, j * D:(j + 1) * D],
                               scalar=col(thr, 2 + t), in1=qo[:, j * D:(j + 1) * D],
                               op0=ALU.is_lt, op1=ALU.subtract),
                           ("s_v", 1) if j == 3 else None)
            P.mark(f"v:stt:{k}", "s_v")
            P.wait("gpsimd", "s_v", f"v:stt:{k}")
            if k < ndt:
                dst = scq1.ap()[:, :, k, :]
            else:
                gg = k - ndt
                dst = scq2.ap()[:, :, gg * 4:(gg + 1) * 4, :]
            P.emit("gpsimd", lambda g, wqo=wqo, dst=dst:
                   g.dma_start(out=dst, in_=wqo), (gwq, 16))
            P.mark(f"st:wq:{k}", gwq)

        def phaseA_chunk(c):
            if c > 0:
                P.wait("scalar", "s_v", f"v:xn:{c - 1}")
                P.emit("scalar", lambda g, c=c: g.dma_start(
                    out=xa[:, :], in_=x_h[c * C:(c + 1) * C, :].rearrange("(tt p) d -> p tt d", p=128)),
                    ("s_xa", 16))
                P.mark(f"xa:{c}", "s_xa")

            P.wait("vector", "s_xa", f"xa:{c}")
            for tt in range(TT):
                P.emit("vector", lambda g, tt=tt: g.reduce_sum(
                    out=col(mcol, tt), in_=xa[:, tt * D:(tt + 1) * D], axis=AX.X),
                    ("s_v", 1) if tt == TT - 1 else None)
            P.mark(f"v:red:{c}", "s_v")

            P.wait("scalar", "s_xa", f"xa:{c}")
            if c > 0:
                P.wait("scalar", "s_t", f"t:tp:{c - 1}")
            for tt in range(TT):
                P.emit("scalar", lambda g, tt=tt: g.activation(
                    out=xnb[:, tt * D:(tt + 1) * D], in_=xa[:, tt * D:(tt + 1) * D],
                    func=AF.Square, accum_out=col(sqcol, tt)),
                    ("s_sc", 1) if tt == TT - 1 else None)
            P.mark(f"sc:sq:{c}", "s_sc")

            P.wait("vector", "s_sc", f"sc:sq:{c}")
            P.wait("vector", "s_v", f"v:red:{c}")
            P.emit("vector", lambda g: g.tensor_scalar(
                out=vtmp[:, 0:TT], in0=mcol[:, :], scalar1=1.0 / D, scalar2=None, op0=ALU.mult),
                ("s_v", 1))
            P.mark(f"v:m:{c}", "s_v")
            P.emit("vector", lambda g: g.tensor_scalar(
                out=vtmp[:, TT:2 * TT], in0=sqcol[:, :], scalar1=1.0 / D, scalar2=None, op0=ALU.mult),
                ("s_v", 1))
            P.mark(f"v:ex2:{c}", "s_v")
            P.wait("vector", "s_v", f"v:m:{c}")
            P.emit("vector", lambda g: g.tensor_mul(
                out=vtmp[:, 2 * TT:3 * TT], in0=vtmp[:, 0:TT], in1=vtmp[:, 0:TT]), ("s_v", 1))
            P.mark(f"v:m2:{c}", "s_v")
            P.wait("vector", "s_v", f"v:m2:{c}")
            P.emit("vector", lambda g: g.scalar_tensor_tensor(
                out=vtmp[:, 3 * TT:4 * TT], in0=vtmp[:, 2 * TT:3 * TT], scalar=-1.0,
                in1=vtmp[:, TT:2 * TT], op0=ALU.mult, op1=ALU.add), ("s_v", 1))
            P.mark(f"v:var:{c}", "s_v")
            P.wait("scalar", "s_v", f"v:var:{c}")
            P.emit("scalar", lambda g: g.activation(
                out=vtmp[:, 4 * TT:5 * TT], in_=vtmp[:, 3 * TT:4 * TT], func=AF.Sqrt,
                bias=epst[:, 0:1]), ("s_sc", 1))
            P.mark(f"sc:sqrt:{c}", "s_sc")
            P.wait("vector", "s_sc", f"sc:sqrt:{c}")
            P.emit("vector", lambda g: g.reciprocal(
                out=vtmp[:, 5 * TT:6 * TT], in_=vtmp[:, 4 * TT:5 * TT]), ("s_v", 1))
            P.mark(f"v:rec:{c}", "s_v")
            P.wait("vector", "s_v", f"v:rec:{c}")
            for tt in range(TT):
                P.emit("vector", lambda g, tt=tt: g.tensor_scalar(
                    out=xnb[:, tt * D:(tt + 1) * D], in0=xa[:, tt * D:(tt + 1) * D],
                    scalar1=col(vtmp, tt), scalar2=col(vtmp, 5 * TT + tt),
                    op0=ALU.subtract, op1=ALU.mult),
                    ("s_v", 1) if tt == TT - 1 else None)
            P.mark(f"v:xn:{c}", "s_v")

            if c == NC - 1 and debug:
                P.wait("gpsimd", "s_v", f"v:xn:{c}")
                P.emit("gpsimd", lambda g: g.dma_start(out=dbg_vtmp, in_=vtmp[:, :]), ("g_out", 16))
                P.emit("gpsimd", lambda g: g.dma_start(out=dbg_mcol, in_=mcol[:, :]), ("g_out", 16))
                P.emit("gpsimd", lambda g: g.dma_start(out=dbg_sqcol, in_=sqcol[:, :]), ("g_out", 16))
                P.wait("gpsimd", "g_out", 3 * 16)

            P.wait("tensor", "s_v", f"v:xn:{c}")
            if c == 0:
                P.wait("tensor", "q_id", 16)
            for dt in range(ndt):
                if dt >= 2:
                    P.wait("tensor", "s_v", f"v:tpe:{c}:{dt - 2}")
                elif c > 0:
                    P.wait("tensor", "s_v", f"v:tpe:{c - 1}:{ndt - 2 + dt}")
                ps = psTa[dt % 2]
                for tt in range(TT):
                    P.emit("tensor", lambda g, tt=tt, dt=dt, ps=ps: g.transpose(
                        ps[:, tt * 128:(tt + 1) * 128],
                        xnb[:, tt * D + dt * 128: tt * D + (dt + 1) * 128], idb[:, :]),
                        ("s_t", 1) if tt == TT - 1 else None)
                P.mark(f"t:tpg:{c}:{dt}", "s_t")
                P.wait("vector", "s_t", f"t:tpg:{c}:{dt}")
                if c > 0 and dt == 0:
                    P.wait("vector", "g_X", f"st:xnT:{c - 1}")
                    P.wait("vector", "s_t", f"t:fi:{c - 1}:{ndt - 1}:i")
                P.emit("vector", lambda g, dt=dt, ps=ps: g.tensor_copy(
                    out=xnT[:, dt * C:(dt + 1) * C], in_=ps[:, :]), ("s_v", 1))
                P.mark(f"v:tpe:{c}:{dt}", "s_v")
            P.mark(f"t:tp:{c}", "s_t")

            P.wait("tensor", "s_v", f"v:tpe:{c}:{ndt - 1}")
            if c == 0:
                P.wait("tensor", "s_sc", f"sc:wcast:wi:{ndt - 1}")
                P.wait("scalar", "q_sm", 16)
            for ht in range(ndt):
                pf, pi = psA[ht % 2], psA[2 + ht % 2]
                if ht >= 2:
                    P.wait("tensor", "s_sc", f"sc:ia:{c}:{ht - 2}")
                elif c > 0:
                    P.wait("tensor", "s_sc", f"sc:ia:{c - 1}:{ndt - 2 + ht}")
                for kd in range(ndt):
                    P.emit("tensor", lambda g, kd=kd, ht=ht, pf=pf: g.matmul(
                        pf[:, :], wf[:, kd * D + ht * 128: kd * D + (ht + 1) * 128],
                        xnT[:, kd * C:(kd + 1) * C],
                        start=(kd == 0), stop=(kd == ndt - 1)),
                        ("s_t", 1) if kd == ndt - 1 else None)
                P.mark(f"t:fi:{c}:{ht}:f", "s_t")
                for kd in range(ndt):
                    P.emit("tensor", lambda g, kd=kd, ht=ht, pi=pi: g.matmul(
                        pi[:, :], wi[:, kd * D + ht * 128: kd * D + (ht + 1) * 128],
                        xnT[:, kd * C:(kd + 1) * C],
                        start=(kd == 0), stop=(kd == ndt - 1)),
                        ("s_t", 1) if kd == ndt - 1 else None)
                P.mark(f"t:fi:{c}:{ht}:i", "s_t")

                fb, ib = fa[ht % 2], ia[ht % 2]
                P.wait("scalar", "s_t", f"t:fi:{c}:{ht}:f")
                if ht >= 2:
                    P.wait("scalar", "s_v", f"v:scan:{c}:{ht - 2}")
                elif c > 0:
                    P.wait("scalar", "s_v", f"v:scan:{c - 1}:{ndt - 2 + ht}")
                P.emit("scalar", lambda g, fb=fb, ht=ht: g.activation(
                    out=fb[:, :], in_=psA[ht % 2][:, :], func=AF.Sigmoid,
                    bias=col(bfs, ht)), ("s_sc", 1))
                P.mark(f"sc:f:{c}:{ht}", "s_sc")
                P.wait("scalar", "s_t", f"t:fi:{c}:{ht}:i")
                P.emit("scalar", lambda g, ib=ib, ht=ht: g.activation(
                    out=ib[:, :], in_=psA[2 + ht % 2][:, :], func=AF.Identity,
                    bias=col(bis, ht)), ("s_sc", 1))
                P.mark(f"sc:ia:{c}:{ht}", "s_sc")

                P.wait("vector", "s_sc", f"sc:ia:{c}:{ht}")
                if ht == 0 and c > 0:
                    P.wait("vector", "g_L", f"st:L:{c - 1}")
                    P.wait("vector", "g_P", f"st:P:{c - 1}")
                P.emit("vector", lambda g, fb=fb: g.tensor_scalar(
                    out=ab[:, :], in0=fb[:, :], scalar1=CEPS, scalar2=CHI,
                    op0=ALU.max, op1=ALU.min), None)
                P.emit("vector", lambda g, ib=ib: g.scalar_tensor_tensor(
                    out=nb[:, :], in0=ab[:, :], scalar=1.0, in1=ib[:, :],
                    op0=ALU.subtract, op1=ALU.mult), None)
                initL = 0.0 if c == 0 else col(lcar, ht)
                initP = 1.0 if c == 0 else col(pcar, ht)
                P.emit("vector", lambda g, ht=ht, initL=initL: g.tensor_tensor_scan(
                    out=nL[:, ht * C:(ht + 1) * C], data0=ab[:, :], data1=nb[:, :],
                    initial=initL, op0=ALU.mult, op1=ALU.add), None)
                P.emit("vector", lambda g, ht=ht, initP=initP: g.tensor_tensor_scan(
                    out=Pp[:, ht * C:(ht + 1) * C], data0=ab[:, :], data1=ab[:, :],
                    initial=initP, op0=ALU.mult, op1=ALU.min), ("s_v", 1))
                P.mark(f"v:scan:{c}:{ht}", "s_v")

            P.emit("vector", lambda g: g.tensor_copy(
                out=lcar[:, :], in_=nL.ap().rearrange("p (h t) -> p h t", t=C)[:, :, C - 1]), None)
            P.emit("vector", lambda g: g.tensor_copy(
                out=pcar[:, :], in_=Pp.ap().rearrange("p (h t) -> p h t", t=C)[:, :, C - 1]),
                ("s_v", 1))
            P.mark(f"v:car:{c}", "s_v")

            P.wait("gpsimd", "s_v", f"v:car:{c}")
            P.emit("gpsimd", lambda g, c=c: g.dma_start(out=scL.ap()[c], in_=nL[:, :]), ("g_L", 16))
            P.mark(f"st:L:{c}", "g_L")
            P.emit("gpsimd", lambda g, c=c: g.dma_start(out=scP.ap()[c], in_=Pp[:, :]), ("g_P", 16))
            P.mark(f"st:P:{c}", "g_P")
            P.emit("gpsimd", lambda g, c=c: g.dma_start(out=scX.ap()[c], in_=xnT[:, :]), ("g_X", 16))
            P.mark(f"st:xnT:{c}", "g_X")

        def emit_cc():
            P.emit("gpsimd", lambda g: g.dma_start(out=ccs.ap()[:, 0:ndt], in_=lcar[:, :]), ("g_cca", 16))
            P.emit("gpsimd", lambda g: g.dma_start(out=ccs.ap()[:, ndt:2 * ndt], in_=pcar[:, :]), ("g_ccb", 16))
            P.wait("gpsimd", "g_cca", 16)
            P.wait("gpsimd", "g_ccb", 16)
            P.emit("gpsimd", lambda g: g.collective_compute(
                "AllGather", ALU.bypass, replica_groups=[list(range(8))],
                ins=[ccs.ap().opt()], outs=[ccr.ap().opt()]), ("s_cc", 1))
            P.wait("gpsimd", "s_cc", 1)
            P.emit("gpsimd", lambda g: g.dma_start(
                out=recv[:, :],
                in_=ccr.ap().rearrange("(r p) c -> p r c", p=128)), ("g_recv", 16))
            P.wait("vector", "g_recv", 16)
            P.wait("vector", "q_sm", 16)
            for j in range(8):
                Aj = recv[:, j * 2 * ndt + ndt:(j + 1) * 2 * ndt]
                Bj = recv[:, j * 2 * ndt: j * 2 * ndt + ndt]
                P.emit("vector", lambda g, Aj=Aj, j=j: g.tensor_scalar(
                    out=tmpA[:, :], in0=Aj, scalar1=col(cms, j), scalar2=col(cm1s, j),
                    op0=ALU.mult, op1=ALU.add), None)
                P.emit("vector", lambda g, Bj=Bj, j=j: g.tensor_scalar(
                    out=tmpB[:, :], in0=Bj, scalar1=col(cms, j), scalar2=None,
                    op0=ALU.mult), ("s_v", 1))
                P.mark(f"v:cab:{j}", "s_v")
                P.wait("vector", "s_v", f"v:cab:{j}")
                P.emit("vector", lambda g, j=j: g.tensor_mul(
                    out=tmp2[:, :], in0=ncar2[j % 2][:, :], in1=tmpA[:, :]), ("s_v", 1))
                P.mark(f"v:cm:{j}", "s_v")
                P.wait("vector", "s_v", f"v:cm:{j}")
                P.emit("vector", lambda g, j=j: g.tensor_add(
                    out=ncar2[(j + 1) % 2][:, :], in0=tmp2[:, :], in1=tmpB[:, :]), ("s_v", 1))
                P.mark(f"v:ca:{j}", "s_v")
                P.wait("vector", "s_v", f"v:ca:{j}")
            P.mark("v:ncar", "s_v")

        def phaseB_chunk(c):
            P.wait("sync", "g_X", f"st:xnT:{c}")
            if c == 0:
                P.wait("sync", "g_X", f"st:xnT:{NC - 1}")
                P.wait("sync", "g_L", f"st:L:{NC - 1}")
                P.wait("sync", "g_P", f"st:P:{NC - 1}")
                P.wait("sync", "g_cca", 16)
                P.wait("sync", "g_ccb", 16)
                P.wait("sync", "s_t", f"t:fi:{NC - 1}:{ndt - 1}:i")
            else:
                P.wait("sync", "s_t", f"t:hg:{c - 1}:{nft - 1}")
            P.emit("sync", lambda g, c=c: g.dma_start(out=xnT[:, :], in_=scX.ap()[c]), ("q_xnT", 16))
            P.mark(f"ld:BxnT:{c}", "q_xnT")

            def ld_wq1(g1):
                w1sem = "q_w1a" if g1 % 2 == 0 else "q_w1b"
                if c == 0 and g1 < 2:
                    P.wait("sync", "s_t", f"t:fi:{NC - 1}:{ndt - 1}:i")
                    P.wait("sync", "g_wqa", f"st:wq:{nq - 2}")
                    P.wait("sync", "g_wqb", f"st:wq:{nq - 1}")
                elif g1 >= 2:
                    P.wait("sync", "s_t", f"t:hg:{c}:{(g1 - 2) * G1 + G1 - 1}")
                else:
                    pg = g1 + 2 * ((NG1 - 1 - g1) // 2)
                    P.wait("sync", "s_t", f"t:hg:{c - 1}:{pg * G1 + G1 - 1}")
                P.emit("sync", lambda g, g1=g1: g.dma_start(
                    out=wf[:, (g1 % 2) * (G1 * ndt * 128):(g1 % 2 + 1) * (G1 * ndt * 128)],
                    in_=scq1.ap()[:, g1 * G1:(g1 + 1) * G1, :, :]), (w1sem, 16))
                P.mark(f"ld:wq1:{c}:{g1}", w1sem)

            def ld_wq2(dt):
                w2sem = "q_w2a" if dt % 2 == 0 else "q_w2b"
                if c == 0 and dt < 2:
                    P.wait("sync", "s_t", f"t:fi:{NC - 1}:{ndt - 1}:i")
                    P.wait("sync", "g_wqa", f"st:wq:{nq - 2}")
                    P.wait("sync", "g_wqb", f"st:wq:{nq - 1}")
                elif dt >= 2:
                    P.wait("sync", "s_t", f"t:o:{c}:{dt - 2}")
                else:
                    P.wait("sync", "s_t", f"t:o:{c - 1}:{ndt - 2 + dt}")
                P.emit("sync", lambda g, dt=dt: g.dma_start(
                    out=wi[:, (dt % 2) * (nft * 128):(dt % 2 + 1) * (nft * 128)],
                    in_=scq2.ap()[:, dt, :, :]), (w2sem, 16))
                P.mark(f"ld:wq2:{c}:{dt}", w2sem)

            ld_wq1(0)
            if NG1 > 1:
                ld_wq1(1)
            if c > 0:
                P.wait("sync", "s_v", f"v:sq:{c - 1}")
            P.wait("sync", "g_L", f"st:L:{c}")
            P.emit("sync", lambda g, c=c: g.dma_start(out=nL[:, :], in_=scL.ap()[c]), ("q_nL", 16))
            P.mark(f"ld:BnL:{c}", "q_nL")
            P.wait("sync", "g_P", f"st:P:{c}")
            P.emit("sync", lambda g, c=c: g.dma_start(out=Pp[:, :], in_=scP.ap()[c]), ("q_Pp", 16))
            P.mark(f"ld:BPp:{c}", "q_Pp")
            for dt in range(min(2, ndt)):
                ld_wq2(dt)
            for g1 in range(2, NG1):
                ld_wq1(g1)
            for dt in range(2, ndt):
                ld_wq2(dt)
            if c == 0:
                P.wait("sync", "s_v", f"v:xn:{NC - 1}")
            else:
                P.wait("sync", "s_v", f"v:cmb:{c - 1}:{TT - 1}")
            P.emit("sync", lambda g, c=c: g.dma_start(
                out=xa[:, :], in_=x_h[c * C:(c + 1) * C, :].rearrange("(tt p) d -> p tt d", p=128)),
                ("q_xa", 16))
            P.mark(f"ld:Bxa:{c}", "q_xa")

            P.wait("vector", "q_nL", f"ld:BnL:{c}")
            P.wait("vector", "q_Pp", f"ld:BPp:{c}")
            if c == 0:
                P.wait("vector", "s_v", "v:ncar")
                P.wait("vector", "s_t", f"t:tp:{NC - 1}")
            for dt in range(ndt):
                P.emit("vector", lambda g, dt=dt: g.scalar_tensor_tensor(
                    out=gstage[:, dt * C:(dt + 1) * C], in0=Pp[:, dt * C:(dt + 1) * C],
                    scalar=col(ncar, dt), in1=nL[:, dt * C:(dt + 1) * C],
                    op0=ALU.mult, op1=ALU.add),
                    ("s_v", 1) if dt == ndt - 1 else None)
            P.mark(f"v:sq:{c}", "s_v")

            P.wait("tensor", "q_xnT", f"ld:BxnT:{c}")
            for m in range(nft):
                ph = psA[m % 2]
                P.wait("tensor", "q_w1a" if (m // G1) % 2 == 0 else "q_w1b", f"ld:wq1:{c}:{m // G1}")
                if m >= 2:
                    P.wait("tensor", "s_sc", f"sc:h:{c}:{m - 2}")
                elif c > 0:
                    P.wait("tensor", "s_sc", f"sc:h:{c - 1}:{nft - 2 + m}")
                else:
                    P.wait("tensor", "s_sc", f"sc:ia:{NC - 1}:{ndt - 2 + m}")
                base = ((m // G1) % 2) * (G1 * ndt * 128) + (m % G1) * ndt * 128
                for kd in range(ndt):
                    P.emit("tensor", lambda g, kd=kd, base=base, ph=ph: g.matmul(
                        ph[:, :], wf[:, base + kd * 128: base + (kd + 1) * 128],
                        xnT[:, kd * C:(kd + 1) * C],
                        start=(kd == 0), stop=(kd == ndt - 1)),
                        ("s_t", 1) if kd == ndt - 1 else None)
                P.mark(f"t:hg:{c}:{m}", "s_t")
                P.wait("scalar", "s_t", f"t:hg:{c}:{m}")
                if c == 0 and m == 0:
                    P.wait("scalar", "g_wqa", f"st:wq:{nq - 2}")
                    P.wait("scalar", "g_wqb", f"st:wq:{nq - 1}")
                    P.wait("scalar", "s_v", f"v:stt:{nq - 1}")
                P.emit("scalar", lambda g, m=m, ph=ph: g.activation(
                    out=hsb[:, m * C:(m + 1) * C], in_=ph[:, :], func=AF.Relu,
                    scale=-1.0, bias=col(b1s, m)), ("s_sc", 1))
                P.mark(f"sc:h:{c}:{m}", "s_sc")

            for dt in range(ndt):
                po = psA[2 + dt % 2]
                P.wait("tensor", "q_w2a" if dt % 2 == 0 else "q_w2b", f"ld:wq2:{c}:{dt}")
                if dt == 0:
                    P.wait("tensor", "s_sc", f"sc:h:{c}:{nft - 1}")
                if dt >= 2:
                    P.wait("tensor", "s_sc", f"sc:g:{c}:{dt - 2}")
                elif c > 0:
                    P.wait("tensor", "s_sc", f"sc:g:{c - 1}:{ndt - 2 + dt}")
                else:
                    P.wait("tensor", "s_sc", f"sc:ia:{NC - 1}:{ndt - 2 + dt}")
                wbase = (dt % 2) * (nft * 128)
                for kf in range(nft):
                    P.emit("tensor", lambda g, kf=kf, wbase=wbase, po=po: g.matmul(
                        po[:, :], wi[:, wbase + kf * 128: wbase + (kf + 1) * 128],
                        hsb[:, kf * C:(kf + 1) * C],
                        start=(kf == 0), stop=(kf == nft - 1)),
                        ("s_t", 1) if kf == nft - 1 else None)
                P.mark(f"t:o:{c}:{dt}", "s_t")
                P.wait("scalar", "s_t", f"t:o:{c}:{dt}")
                if c == 0 and dt == 0:
                    P.wait("scalar", "s_v", f"v:stt:{nq - 1}")
                if dt >= 1 or c > 0:
                    prev = (c, dt - 1) if dt >= 1 else (c - 1, ndt - 1)
                    P.wait("scalar", "s_v", f"v:qg:{prev[0]}:{prev[1]}")
                if c > 0 and dt == 0:
                    P.wait("scalar", "g_out", f"st:out:{c - 1}")
                    P.wait("scalar", "s_v", f"v:cmb:{c - 1}:{TT - 1}")
                P.emit("scalar", lambda g, dt=dt, po=po: g.activation(
                    out=gsb[:, dt * C:(dt + 1) * C], in_=po[:, :], func=AF.Silu,
                    scale=-1.0, bias=col(b2s, dt)), ("s_sc", 1))
                P.mark(f"sc:g:{c}:{dt}", "s_sc")

                P.wait("vector", "s_sc", f"sc:g:{c}:{dt}")
                P.emit("vector", lambda g, dt=dt: g.tensor_mul(
                    out=xnb[:, dt * C:(dt + 1) * C], in0=gsb[:, dt * C:(dt + 1) * C],
                    in1=gstage[:, dt * C:(dt + 1) * C]), ("s_v", 1))
                P.mark(f"v:qg:{c}:{dt}", "s_v")

            for tt in range(TT):
                P.wait("tensor", "s_v", f"v:qg:{c}:{ndt - 1}")
                if tt >= 2:
                    P.wait("tensor", "s_v", f"v:cmb:{c}:{tt - 2}")
                elif c > 0:
                    P.wait("tensor", "s_v", f"v:cmb:{c - 1}:{TT - 2 + tt}")
                pb = psTb2[tt % 2]
                for ht in range(ndt):
                    P.emit("tensor", lambda g, tt=tt, ht=ht, pb=pb: g.transpose(
                        pb[:, ht * 128:(ht + 1) * 128],
                        xnb[:, ht * C + tt * 128: ht * C + (tt + 1) * 128], idb[:, :]),
                        ("s_t", 1) if ht == ndt - 1 else None)
                P.mark(f"t:qt:{c}:{tt}", "s_t")
                P.wait("vector", "s_t", f"t:qt:{c}:{tt}")
                if tt == 0:
                    P.wait("vector", "q_xa", f"ld:Bxa:{c}")
                P.emit("vector", lambda g, tt=tt, pb=pb: g.tensor_sub(
                    out=gsb[:, tt * D:(tt + 1) * D], in0=xa[:, tt * D:(tt + 1) * D],
                    in1=pb[:, 0:D]), ("s_v", 1))
                P.mark(f"v:cmb:{c}:{tt}", "s_v")

            P.wait("gpsimd", "s_v", f"v:cmb:{c}:{TT - 1}")
            P.emit("gpsimd", lambda g, c=c: g.dma_start(
                out=out_h[c * C:(c + 1) * C, :].rearrange("(tt p) d -> p tt d", p=128),
                in_=gsb[:, :]), ("g_out", 16))
            P.mark(f"st:out:{c}", "g_out")

        P.emit("scalar", lambda g: g.dma_start(
            out=xa[:, :], in_=x_h[0:C, :].rearrange("(tt p) d -> p tt d", p=128)),
            ("s_xa", 16))
        P.mark("xa:0", "s_xa")
        for k in range(ndt):
            emit_w_cast("wf", wfT_h, wf, k)
        for k in range(ndt):
            emit_w_cast("wi", wiT_h, wi, k)
        prologue = ([("g", k) for k in range(nq)] + [("t", 0)]
                    + [("q", k) for k in range(nq)])
        per_slot = -(-len(prologue) // min(NC, 6))
        for c in range(NC):
            phaseA_chunk(c)
            for item in prologue[c * per_slot:(c + 1) * per_slot]:
                if item[0] == "g":
                    emit_gamma(item[1])
                elif item[0] == "t":
                    emit_thresh(c)
                else:
                    emit_quant(item[1])
        for item in prologue[NC * per_slot:]:
            if item[0] == "g":
                emit_gamma(item[1])
            elif item[0] == "t":
                emit_thresh(NC - 1)
            else:
                emit_quant(item[1])
        emit_cc()
        for c in range(NC):
            phaseB_chunk(c)

        P.recs["gpsimd"].append(("wait", "g_out", P.cnt.get("g_out", 0)))

        P.replay(block, semh)

    return nc


_NC_CACHE = {}


def _get_nc():
    if "full" not in _NC_CACHE:
        _NC_CACHE["full"] = build_program()
    return _NC_CACHE["full"]


def kernel(L0_attn_x, L0_attn_W_f, L0_attn_b_f, L0_attn_W_i, L0_attn_b_i,
           L0_ffn_W1, L0_ffn_b1, L0_ffn_W2, L0_ffn_b2):
    D, FF, NT = 1024, 4096, 4096
    ndt, nft = D // 128, FF // 128
    x = np.ascontiguousarray(np.asarray(L0_attn_x, dtype=np.float32))
    wfT = np.ascontiguousarray(np.asarray(L0_attn_W_f, dtype=np.float32).T)
    wiT = np.ascontiguousarray(np.asarray(L0_attn_W_i, dtype=np.float32).T)
    w1T = np.ascontiguousarray(np.asarray(L0_ffn_W1, dtype=np.float32).T)
    w2T = np.ascontiguousarray(np.asarray(L0_ffn_W2, dtype=np.float32).T)
    bfv = np.asarray(L0_attn_b_f, dtype=np.float32).reshape(ndt, 128).T
    biv = np.asarray(L0_attn_b_i, dtype=np.float32).reshape(ndt, 128).T
    b1v = np.asarray(L0_ffn_b1, dtype=np.float32).reshape(nft, 128).T
    b2v = np.asarray(L0_ffn_b2, dtype=np.float32).reshape(ndt, 128).T
    identb = np.eye(128, dtype=ml_dtypes.bfloat16)
    identf = np.eye(128, dtype=np.float32)

    nc = _get_nc()
    in_maps = []
    for k in range(8):
        b, s = k // 4, k % 4
        mask = np.array([1.0 if (j // 4 == k // 4 and j < k) else 0.0 for j in range(8)],
                        dtype=np.float32)
        sm = np.concatenate([
            bfv, biv, b2v, b1v,
            np.tile(mask, (128, 1)), np.tile(1.0 - mask, (128, 1)), identf,
        ], axis=1).astype(np.float32)
        in_maps.append({
            "x": np.ascontiguousarray(x[b, s * NT:(s + 1) * NT, :]),
            "wfT": wfT, "wiT": wiT, "w1T": w1T, "w2T": w2T,
            "smalls": np.ascontiguousarray(sm), "identb": identb,
        })
    res = run_bass_kernel_spmd(nc, in_maps, core_ids=list(range(8)))
    out = np.empty((2, 4 * NT, D), dtype=np.float32)
    for k in range(8):
        b, s = k // 4, k % 4
        out[b, s * NT:(s + 1) * NT, :] = res.results[k]["out"]
    return out
